# version 01
# baseline (speedup 1.0000x reference)
import numpy as np
import ml_dtypes
import contextlib
import concourse.bass as bass
import concourse.mybir as mybir
from concourse.bass import ds
from concourse.bass_utils import run_bass_kernel_spmd
from concourse.alu_op_type import AluOpType as ALU

F32 = mybir.dt.float32
BF16 = mybir.dt.bfloat16
AF = mybir.ActivationFunctionType
AX = mybir.AxisListType
ET = mybir.EngineType

B, Lq, DM = 4, 4096, 2048
TOK, SEG = 2048, 1024
NSEG, NPRE = 2, 2
TCN, TBN, CI = 8, 2, 16
H, HD, FD = 32, 64, 64
NE, FCN, OBN = 10, 11, 4
EPS_LN, EPS_AT = 1e-5, 1e-12
CONST_VALS = [EPS_LN, -0.5, 1.0 / DM, -1.0, EPS_AT, -1e30, 1e-20, 2.0]

ENAMES = {"pe": ET.PE, "dv": ET.DVE, "sc": ET.Activation, "gp": ET.Pool, "sp": ET.SP}
_DRY = [False]


class _MockInst:
    def then_inc(self, *a, **k):
        return self


_MOCK = _MockInst()


class Eng:
    def __init__(self, real):
        self._real = real

    def __getattr__(self, name):
        if _DRY[0]:
            if name == "snap":
                return lambda *a, **k: 0
            return lambda *a, **k: _MOCK
        return getattr(self._real, name)


class S:
    def __init__(self, nc, es, name):
        self.h = es.enter_context(nc.semaphore(name))
        self.n = 0
        self.name = name

    def inc(self, inst, k=1):
        self.n += k
        if inst is not None and not isinstance(inst, _MockInst):
            inst.then_inc(self.h, k)
        return self.n


class KB:
    def __init__(self, nc, es):
        self.nc = nc
        self.es = es
        self.sems = []
        attr = {"pe": "tensor", "dv": "vector", "sc": "scalar", "gp": "gpsimd", "sp": "sync"}
        self.eng = {n: Eng(getattr(nc, a)) for n, a in attr.items()}
        self.real = {n: getattr(nc, a) for n, a in attr.items()}
        self.rW = {}
        self.tmp = {}

    def sem(self, name):
        s = S(self.nc, self.es, name)
        self.sems.append(s)
        return s

    def declare_waiters(self, pairs):
        for en, sem in pairs:
            r = self.real[en].alloc_register(f"rW_{en}_{sem.name}")
            self.real[en].reg_mov(r, 0)
            self.rW[(en, sem)] = r
        for en in ENAMES:
            t = self.real[en].alloc_register(f"tmp_{en}")
            self.tmp[en] = t

    def wait(self, en, sem, off=0):
        self.waitabs(en, sem, sem.n + off)

    def waitabs(self, en, sem, static):
        if _DRY[0]:
            return
        e = self.real[en]
        key = (en, sem)
        if key in self.rW:
            e.reg_alu(self.tmp[en], self.rW[key], static, mybir.AluOpType.add)
            e.wait_ge(sem.h, self.tmp[en])
        else:
            assert static >= 0
            e.wait_ge(sem.h, static)

    def forloop(self, n, engines, body, extra=None):
        extra = extra or {}
        snap = [s.n for s in self.sems]
        if _DRY[0]:
            body()
            for s, s0 in zip(self.sems, snap):
                s.n = s0 + n * (s.n - s0)
            return
        _DRY[0] = True
        body()
        real_d = {s: s.n - s0 for s, s0 in zip(self.sems, snap)}
        for s, s0 in zip(self.sems, snap):
            s.n = s0
        _DRY[0] = False
        deltas = dict(real_d)
        for s, d in extra.items():
            deltas[s] = deltas.get(s, 0) + d

        nc = self.nc
        engs = [ENAMES[e] for e in engines]
        loop_id = nc.next_id()
        ls, le = f"kf_{loop_id}_loop", f"kf_{loop_id}_end"
        registers = nc.alloc_registers(f"kf_{loop_id}_i", engines=engs)
        nc.regs_mov(registers, 0)
        nc.br(ls, engines=engs)
        with nc.body(ls, valid_engines=engs):
            body()
            for en in engines:
                for (wen, sem), r in self.rW.items():
                    if wen == en and deltas.get(sem, 0):
                        self.real[en].reg_alu(r, r, deltas[sem], mybir.AluOpType.add)
            nc.regs_alu(registers, registers, 1, op=mybir.AluOpType.add)
            nc.br_lt(registers, n, on_true=ls, on_false=le, engines=engs)
        nc.switch_bb(le)
        for e, h in zip(registers.engines, registers.handles):
            nc.engines[e].free_register(h)
        for en in engines:
            for (wen, sem), r in self.rW.items():
                if wen == en and deltas.get(sem, 0):
                    self.real[en].reg_alu(r, r, -n * deltas[sem], mybir.AluOpType.add)
        for s, s0 in zip(self.sems, snap):
            s.n = s0 + n * real_d.get(s, 0)

    def barrier(self):
        if not _DRY[0]:
            self.nc.all_engine_barrier()


def build_nc(parts="PABCRDS"):
    nc = bass.Bass("TRN2", target_bir_lowering=False, debug=False)
    dt = nc.dram_tensor
    x_own = dt("x_own", [TOK, DM], F32, kind="ExternalInput").ap()
    x_pre = dt("x_pre", [TOK, DM], F32, kind="ExternalInput").ap()
    flagd = dt("flagd", [128, 1], F32, kind="ExternalInput").ap()
    wvo_t = dt("wvo_t", [8, 128, CI * 512], F32, kind="ExternalInput").ap()
    wpr_t = dt("wpr_t", [16 + NE * 12, 128, 4352], F32, kind="ExternalInput").ap()
    wd_t = dt("wd_t", [NE * OBN, 128, FCN * 512], F32, kind="ExternalInput").ap()
    gw_t = dt("gw_t", [128, CI * 8], F32, kind="ExternalInput").ap()
    ident = dt("ident", [128, 128], BF16, kind="ExternalInput").ap()
    maskut = dt("maskut", [128, 128], BF16, kind="ExternalInput").ap()
    cvals = dt("cvals", [128, 8], F32, kind="ExternalInput").ap()
    out = dt("out", [TOK, DM], F32, kind="ExternalOutput").ap()
    dscr = dt("dscr", [NE * 1024, DM], F32, kind="Internal").ap()

    es = contextlib.ExitStack()
    sbt = lambda name, shape, dty: es.enter_context(nc.sbuf_tensor(name, shape, dty))
    pst = lambda name, shape, dty: es.enter_context(nc.psum_tensor(name, shape, dty))

    xn_fm = sbt("xn_fm", [128, CI * SEG], BF16)
    yat = sbt("yat", [128, 16 * SEG], BF16)
    h_fm = sbt("h_fm", [128, FCN * SEG], BF16)
    v_ext = sbt("v_ext", [128, 8 * TCN * 65], BF16)
    qk_fm = sbt("qk_fm", [128, 2 * SEG], BF16)
    x_sb = sbt("x_sb", [128, DM], F32)
    wvb = sbt("wvb", [128, CI * 512], BF16)
    wqg = sbt("wqg", [128, 2 * 4352], BF16)
    wdb = sbt("wdb", [128, 2 * FCN * 512], BF16)
    stg_c = sbt("stg_c", [128, 2 * 512], F32)
    stg_d = sbt("stg_d", [128, 4 * 1024], F32)
    gw_sb = sbt("gw_sb", [128, CI * 8], BF16)
    cw_sb = sbt("cw_sb", [128, NE * TCN], F32)
    cwcur = sbt("cwcur", [128, 2 * TCN], F32)
    S32 = sbt("S32", [64, H * 65], F32)
    S_cur = sbt("S_cur", [64, 8 * 65], F32)
    S16c = sbt("S16c", [64, 65], BF16)
    y_cur = sbt("y_cur", [128, 4 * SEG], BF16)
    sg_sb = sbt("sg_sb", [128, 512], F32)
    id_sb = sbt("id_sb", [128, 128], BF16)
    mk_sb = sbt("mk_sb", [128, 128], BF16)
    ones64 = sbt("ones64", [1, 64], F32)
    flag_sb = sbt("flag_sb", [128, 1], F32)
    cv_sb = sbt("cv_sb", [128, 8], F32)
    ex_q = sbt("ex_q", [128, 64], F32)
    ex_k = sbt("ex_k", [128, 64], F32)
    fq_tm = sbt("fq_tm", [128, 64], BF16)
    fk_tm = sbt("fk_tm", [128, 64], BF16)
    fqT = sbt("fqT", [64, 128], BF16)
    fkT = sbt("fkT", [64, 128], BF16)
    atm = sbt("atm", [128, 128], BF16)
    rd_f = sbt("rd_f", [1, 128], F32)
    rden = sbt("rden", [64, 128], F32)
    colA = sbt("colA", [128, 1], F32)
    colB = sbt("colB", [128, 1], F32)
    colC = sbt("colC", [128, 1], F32)
    colD = sbt("colD", [128, 1], F32)
    r_s = sbt("r_s", [128, 8], F32)
    r_s2 = sbt("r_s2", [128, 8], F32)

    for i, v in enumerate(CONST_VALS):
        nc.const_aps.register(F32, v, cv_sb[:, i:i + 1])

    P0 = pst("P0", [128, 512], F32)
    P1 = pst("P1", [128, 512], F32)
    P2 = pst("P2", [128, 512], F32)
    P3 = pst("P3", [128, 512], F32)
    P4 = pst("P4", [128, 512], F32)
    P5 = pst("P5", [128, 512], F32)
    P6 = pst("P6", [64, 512], F32)
    P7 = pst("P7", [128, 1024], BF16)

    k = KB(nc, es)
    LV = k.sem("LV")
    LQ0 = k.sem("LQ0")
    LQ1 = k.sem("LQ1")
    LQs = [LQ0, LQ1]
    LD0 = k.sem("LD0")
    LD1 = k.sem("LD1")
    LDs = [LD0, LD1]
    LXA = k.sem("LXA")
    FXA = k.sem("FXA")
    FWV = k.sem("FWV")
    FWQ = k.sem("FWQ")
    FWD = k.sem("FWD")
    PEm = k.sem("PEm")
    PEn = k.sem("PEn")
    DVm = k.sem("DVm")
    SCm = k.sem("SCm")
    STC0 = k.sem("STC0")
    STC1 = k.sem("STC1")
    STCs = [STC0, STC1]
    STD = k.sem("STD")
    SPD = k.sem("SPD")
    SGC = k.sem("SGC")
    SGD = k.sem("SGD")
    LC = k.sem("LC")

    k.declare_waiters([
        ("pe", LV), ("pe", LQ0), ("pe", LQ1), ("pe", LD0), ("pe", LD1),
        ("pe", DVm), ("pe", SCm), ("pe", FXA), ("pe", SGC), ("pe", SGD),
        ("dv", PEm), ("dv", PEn), ("dv", SCm), ("dv", LXA), ("dv", SPD), ("dv", FWD), ("dv", DVm),
        ("dv", FWV), ("dv", FWQ),
        ("sc", DVm), ("sc", PEm), ("sc", LXA), ("sc", STC0), ("sc", STC1), ("sc", SCm), ("sc", FWV),
        ("gp", FWV), ("gp", FWQ), ("gp", FWD), ("gp", SPD), ("gp", STD), ("gp", STC0), ("gp", STC1),
        ("sp", FXA), ("sp", SGC), ("sp", SGD), ("sp", SPD),
    ])
    pe, dv, sc, gp, sp = (k.eng[n] for n in ("pe", "dv", "sc", "gp", "sp"))
    PE_, DV_, SC_, GP_, SP_ = (k.real[n] for n in ("pe", "dv", "sc", "gp", "sp"))

    r_uvo = GP_.alloc_register("r_uvo")
    r_uqg = GP_.alloc_register("r_uqg")
    r_uwd = GP_.alloc_register("r_uwd")
    r_so = GP_.alloc_register("r_so")
    r_xrow = SP_.alloc_register("r_xrow")
    r_cst = SP_.alloc_register("r_cst")
    r_sd = SP_.alloc_register("r_sd")
    r_tcA = DV_.alloc_register("r_tcA")
    r_cw = DV_.alloc_register("r_cw")
    r_g8s = SC_.alloc_register("r_g8s")
    r_g8y = SC_.alloc_register("r_g8y")

    wvo_th = wvo_t.tensor
    wpr_th = wpr_t.tensor
    wd_th = wd_t.tensor
    xn4 = xn_fm[:].rearrange("p (c t) -> p c t", c=CI)
    p7v = P7[:].rearrange("p (j e) -> p j e", j=8)
    vve = v_ext[:].rearrange("p (a b c) -> p a b c", a=8, b=TCN, c=65)
    cwv = cw_sb[:].rearrange("p (a b) -> p a b", a=NE, b=TCN)

    with es:
        LC.inc(gp.dma_start(out=cv_sb[:], in_=cvals), 16)
        LC.inc(gp.dma_start(out=id_sb[:], in_=ident), 16)
        LC.inc(gp.dma_start(out=mk_sb[:], in_=maskut), 16)
        LC.inc(gp.dma_start(out=gw_sb[:], in_=gw_t), 16)
        LC.inc(gp.dma_start(out=flag_sb[:], in_=flagd), 16)
        sp.sem_inc(FWV.h, 1)
        FWV.n += 1
        sp.sem_inc(FWQ.h, 2)
        FWQ.n += 2
        sp.sem_inc(FWD.h, 2)
        FWD.n += 2

        sp.sem_inc(SGC.h, 2)
        SGC.n += 2
        dv.memset(S32[:], 0.0)
        dv.memset(vve[:, :, :, 64:65], 1.0)
        dv.memset(ones64[:], 1.0)
        dv.memset(cwv[:, 8:10, :], 1.0)
        for r in (r_uvo, r_uqg, r_uwd, r_so):
            gp.reg_mov(r, 0)
        for r in (r_xrow, r_cst, r_sd):
            sp.reg_mov(r, 0)
        for r in (r_tcA, r_cw):
            dv.reg_mov(r, 0)
        for r in (r_g8s, r_g8y):
            sc.reg_mov(r, 0)
        k.barrier()

        def pool_stream(n, src_th, elen, reg, sem_loads, sem_free, dests):

            def body():
                u0 = sum(s.n for s in sem_loads) // 16
                for j, (dst, sl) in enumerate(zip(dests, sem_loads)):
                    k.waitabs("gp", sem_free, u0 + j + 1)
                    ap = bass.AP(src_th, reg, [[elen, 128], [1, elen]])
                    sl.inc(gp.dma_start(out=dst, in_=ap), 16)
                    gp.reg_add(reg, reg, 128 * elen)

            k.forloop(n, ["gp"], body, extra={sem_free: len(dests)})

        def poolB_stream():

            def body():
                uv0 = LV.n // 16
                k.waitabs("gp", FWV, uv0 + 1)
                apv = bass.AP(wvo_th, r_uvo, [[CI * 512, 128], [1, CI * 512]])
                LV.inc(gp.dma_start(out=wvb[:], in_=apv), 16)
                gp.reg_add(r_uvo, r_uvo, 128 * CI * 512)
                uq0 = (LQ0.n + LQ1.n) // 16

                def qpair():
                    for j in (0, 1):
                        k.waitabs("gp", FWQ, uq0 + j + 1)
                        apq = bass.AP(wpr_th, r_uqg, [[4352, 128], [1, 4352]])
                        LQs[j].inc(gp.dma_start(out=wqg[:, j * 4352:(j + 1) * 4352], in_=apq), 16)
                        gp.reg_add(r_uqg, r_uqg, 128 * 4352)

                k.forloop(2, ["gp"], qpair, extra={FWQ: 2})

            k.forloop(4, ["gp"], body, extra={FWV: 1, FWQ: 4})


        def phaseA(xsrc):
            def body():
                k.wait("sp", FXA, 0)
                rr = sp.snap(r_xrow, min_val=0, max_val=TOK - 128)
                lx = LXA.inc(sp.dma_start(out=x_sb[:], in_=xsrc[ds(rr, 128), :]), 16)
                sp.reg_add(r_xrow, r_xrow, 128)
                k.waitabs("sc", LXA, lx)
                sq = SCm.inc(sc.activation(h_fm[:, 0:2048], x_sb[:], AF.Square, accum_out=colA[:]))
                k.waitabs("dv", LXA, lx)
                DVm.inc(dv.tensor_reduce(colB[:], x_sb[:], AX.X, ALU.add))
                k.wait("dv", DVm, 0)
                DVm.inc(dv.tensor_scalar_mul(colB[:], colB[:], 1.0 / DM))
                k.wait("dv", DVm, 0)
                k.waitabs("dv", SCm, sq)
                dv.tensor_scalar_mul(colC[:], colA[:], 1.0 / DM)
                DVm.inc(dv.tensor_tensor(colD[:], colB[:], colB[:], ALU.mult))
                k.wait("dv", DVm, 0)
                d3 = DVm.inc(dv.tensor_tensor(colC[:], colC[:], colD[:], ALU.subtract))
                k.waitabs("sc", DVm, d3)
                s2 = SCm.inc(sc.activation(colC[:], colC[:], AF.Ln, bias=EPS_LN))
                k.waitabs("sc", SCm, s2)
                s3 = SCm.inc(sc.activation(colC[:], colC[:], AF.Exp, scale=-0.5))
                k.waitabs("dv", SCm, s3)
                d4 = DVm.inc(dv.tensor_scalar(colD[:], colB[:], colC[:], -1.0,
                                              op0=ALU.mult, op1=ALU.mult))
                k.waitabs("sc", DVm, d4)
                k.wait("sc", PEm, 0)
                fx = FXA.inc(sc.activation(h_fm[:, 2048:4096], x_sb[:], AF.Identity,
                                           bias=colD[:], scale=colC[:]))
                toff = dv.snap(r_tcA, min_val=0, max_val=SEG - 128)
                for half in (0, 1):
                    k.waitabs("pe", FXA, fx)
                    k.wait("pe", DVm, 0)
                    inst = None
                    for j in range(8):
                        c = half * 8 + j
                        inst = pe.transpose(P7[:, j * 128:(j + 1) * 128],
                                            h_fm[:, 2048 + c * 128:2048 + (c + 1) * 128],
                                            id_sb[:])
                    p1 = PEm.inc(inst)
                    k.waitabs("dv", PEm, p1)
                    DVm.inc(dv.tensor_copy(xn4[:, half * 8:(half + 1) * 8, ds(toff, 128)],
                                           p7v[:, :, :]))
                dv.reg_add(r_tcA, r_tcA, 128)

            k.forloop(TCN, ["pe", "dv", "sc", "sp"], body)
            if not _DRY[0]:
                DV_.reg_add(r_tcA, r_tcA, -SEG)

        def vproj(lv_abs):
            for tc in range(TCN):
                bank = P0 if tc % 2 == 0 else P1
                bk4 = bank[:].rearrange("p (a b) -> p a b", a=8)
                k.waitabs("pe", LV, lv_abs)
                k.wait("pe", DVm, 0)
                for c in range(CI):
                    inst = pe.matmul(bank[:, 0:512],
                                     xn_fm[:, c * SEG + tc * 128:c * SEG + (tc + 1) * 128],
                                     wvb[:, c * 512:(c + 1) * 512],
                                     start=(c == 0), stop=(c == CI - 1))
                if tc == TCN - 1:
                    fv = FWV.inc(inst)
                    k.waitabs("dv", FWV, fv)
                else:
                    p1 = PEm.inc(inst)
                    k.waitabs("dv", PEm, p1)
                DVm.inc(dv.tensor_copy(vve[:, :, tc, 0:64], bk4[:, :, :]))

        def qkproj(lq_abs, slot, prime):
            ms = (1,) if prime else (0, 1)
            for m in ms:
                for tb in range(TBN):
                    bank = P0 if tb == 0 else P1
                    k.waitabs("pe", LQs[slot], lq_abs)
                    k.wait("pe", DVm, 0)
                    for c in range(CI):
                        inst = pe.matmul(bank[:, 0:512],
                                         wqg[:, slot * 4352 + m * 2048 + c * 128:
                                             slot * 4352 + m * 2048 + (c + 1) * 128],
                                         xn_fm[:, c * SEG + tb * 512:c * SEG + (tb + 1) * 512],
                                         start=(c == 0), stop=(c == CI - 1))
                    p1 = PEm.inc(inst)
                    k.waitabs("dv", PEm, p1)
                    DVm.inc(dv.tensor_copy(qk_fm[:, m * SEG + tb * 512:m * SEG + (tb + 1) * 512],
                                           bank[:, 0:512]))

        def mixer_head(h8, prime, last_of_pair):
            hb = (h8 % 2) * 64
            hp_local = h8 // 2
            slot = hp_local % 2
            hloc = h8 % 2
            wfq_ap = wqg[hb:hb + 64, slot * 4352 + 4096 + hloc * 64:
                         slot * 4352 + 4096 + hloc * 64 + 64]
            wfk_ap = wqg[hb:hb + 64, slot * 4352 + 4224 + hloc * 64:
                         slot * 4352 + 4224 + hloc * 64 + 64]
            sA = S_cur[0:64, h8 * 65:(h8 + 1) * 65]
            if not prime:
                k.wait("dv", SCm, 0)
                k.wait("dv", DVm, 0)
                DVm.inc(dv.tensor_copy(S16c[:], sA))
            for c in range(TCN):
                k.wait("pe", DVm, 0)
                k.wait("pe", SCm, 0)
                if not prime:
                    pe.matmul(P4[:, 0:64],
                              qk_fm[hb:hb + 64, c * 128:(c + 1) * 128],
                              wfq_ap, start=True, stop=True)
                p1 = PEm.inc(pe.matmul(P4[:, 64:128],
                                       qk_fm[hb:hb + 64, SEG + c * 128:SEG + (c + 1) * 128],
                                       wfk_ap, start=True, stop=True))
                k.waitabs("dv", PEm, p1)
                if not prime:
                    dv.tensor_reduce(colA[:], P4[:, 0:64], AX.X, ALU.max, negate=True)
                d1 = DVm.inc(dv.tensor_reduce(colB[:], P4[:, 64:128], AX.X, ALU.max, negate=True))
                k.waitabs("sc", DVm, d1)
                if not prime:
                    sc.activation(ex_q[:], P4[:, 0:64], AF.Exp, bias=colA[:])
                s1 = SCm.inc(sc.activation(ex_k[:], P4[:, 64:128], AF.Exp, bias=colB[:]))
                k.waitabs("dv", SCm, s1)
                if not prime:
                    dv.tensor_reduce(colA[:], ex_q[:], AX.X, ALU.add)
                DVm.inc(dv.tensor_reduce(colB[:], ex_k[:], AX.X, ALU.add))
                k.wait("dv", DVm, 0)
                if not prime:
                    dv.reciprocal(colA[:], colA[:])
                DVm.inc(dv.reciprocal(colB[:], colB[:]))
                k.wait("dv", DVm, 0)
                if prime:
                    dfk = DVm.inc(dv.tensor_scalar(fk_tm[:], ex_k[:], colB[:], flag_sb[:],
                                                   op0=ALU.mult, op1=ALU.mult))
                else:
                    dv.tensor_scalar_mul(fq_tm[:], ex_q[:], colA[:])
                    dfk = DVm.inc(dv.tensor_scalar_mul(fk_tm[:], ex_k[:], colB[:]))

                if not prime:
                    k.waitabs("pe", DVm, dfk)
                    pe.transpose(P7[0:64, 0:128], fq_tm[:], id_sb[:])
                    p2 = PEm.inc(pe.transpose(P7[0:64, 128:256], fk_tm[:], id_sb[:]))
                    k.waitabs("sc", PEm, p2)
                    sc.copy(fqT[:], P7[0:64, 0:128])
                    s2 = SCm.inc(sc.copy(fkT[:], P7[0:64, 128:256]))
                    k.waitabs("pe", SCm, s2)
                    p3 = PEm.inc(pe.matmul(P5[:, 0:128], fkT[:], fqT[:], start=True, stop=True))
                    k.waitabs("dv", PEm, p3)
                    d4 = DVm.inc(dv.tensor_tensor(atm[:], P5[:, 0:128], mk_sb[:], ALU.mult))
                    k.waitabs("pe", DVm, d4)
                    pe.matmul(P5[0:64, 128:256],
                              v_ext[:, (h8 * TCN + c) * 65:(h8 * TCN + c) * 65 + 64],
                              atm[:], start=True, stop=False)
                    pe.matmul(P5[0:64, 128:256], S16c[0:64, 0:64], fqT[:], start=False, stop=True)
                    pe.matmul(P4[0:1, 128:256], mk_sb[:, 127:128], atm[:], start=True, stop=False)
                    p4 = PEm.inc(pe.matmul(P4[0:1, 128:256], S16c[0:64, 64:65], fqT[:],
                                           start=False, stop=True))
                    k.waitabs("dv", PEm, p4)
                    DVm.inc(dv.tensor_scalar_add(rd_f[:], P4[0:1, 128:256], EPS_AT))
                    k.wait("dv", DVm, 0)
                    d6 = DVm.inc(dv.reciprocal(rd_f[:], rd_f[:]))
                    k.waitabs("pe", DVm, d6)
                    p5 = PEm.inc(pe.matmul(P5[0:64, 256:384], ones64[:], rd_f[:],
                                           start=True, stop=True))
                    k.waitabs("dv", PEm, p5)
                    DVm.inc(dv.tensor_copy(rden[:], P5[0:64, 256:384]))
                    k.wait("dv", DVm, 0)
                    k.wait("dv", SCm, 0)
                    DVm.inc(dv.tensor_tensor(
                        y_cur[hb:hb + 64, hp_local * SEG + c * 128:hp_local * SEG + (c + 1) * 128],
                        P5[0:64, 128:256], rden[:], ALU.mult))
                k.waitabs("pe", DVm, dfk)
                inst6 = pe.matmul(P6[0:64, 0:65], fk_tm[:],
                                  v_ext[:, (h8 * TCN + c) * 65:(h8 * TCN + c + 1) * 65],
                                  start=True, stop=True)
                if last_of_pair and c == TCN - 1:
                    f6 = FWQ.inc(inst6)
                    k.waitabs("dv", FWQ, f6)
                else:
                    p6 = PEm.inc(inst6)
                    k.waitabs("dv", PEm, p6)
                k.wait("dv", SCm, 0)
                d8 = DVm.inc(dv.tensor_tensor(sA, sA, P6[0:64, 0:65], ALU.add))
                if not prime:
                    k.waitabs("dv", DVm, d8)
                    DVm.inc(dv.tensor_copy(S16c[:], sA))

        def phaseB(prime, lv_base, lq_base):
            def body():
                k.wait("sc", PEm, 0)
                k.wait("sc", DVm, 0)
                k.wait("sc", SCm, 0)
                ssnap = sc.snap(r_g8s, min_val=0, max_val=H * 65 - 520)
                s32slice = S32[0:64, ds(ssnap, 520)]
                SCm.inc(sc.copy(S_cur[:], s32slice))
                vproj(lv_base + 16)
                for hp in range(4):
                    qkproj(lq_base[hp % 2] + 16 * (hp // 2 + 1), hp % 2, prime)
                    for h in (0, 1):
                        mixer_head(hp * 2 + h, prime, h == 1)
                k.wait("sc", DVm, 0)
                SCm.inc(sc.copy(s32slice, S_cur[:]))
                if not prime:
                    ysnap = sc.snap(r_g8y, min_val=0, max_val=12288)
                    SCm.inc(sc.copy(yat[:, ds(ysnap, 4096)], y_cur[:]))
                sc.reg_add(r_g8s, r_g8s, 520)
                sc.reg_add(r_g8y, r_g8y, 4096)

            k.forloop(4, ["pe", "dv", "sc"], body, extra={LV: 16, LQ0: 32, LQ1: 32})
            if not _DRY[0]:
                SC_.reg_add(r_g8s, r_g8s, -2080)
                SC_.reg_add(r_g8y, r_g8y, -16384)

        def phaseC(lv_base):
            def body():
                for tc in range(TCN):
                    par = tc % 2
                    bank = P0 if par == 0 else P1
                    k.waitabs("pe", LV, lv_base + 16 * 5)
                    k.wait("pe", SCm, 0)
                    k.wait("pe", SGC, -1)
                    k.wait("pe", DVm, 0)
                    for c in range(CI):
                        inst = pe.matmul(bank[:, 0:512],
                                         yat[:, c * SEG + tc * 128:c * SEG + (tc + 1) * 128],
                                         wvb[:, c * 512:(c + 1) * 512],
                                         start=(c == 0), stop=(c == CI - 1))
                    if tc == TCN - 1:
                        fv = FWV.inc(inst)
                        k.waitabs("sc", FWV, fv)
                    else:
                        p1 = PEm.inc(inst)
                        k.waitabs("sc", PEm, p1)
                    k.wait("sc", STCs[par], 0)
                    sg = SGC.inc(sc.copy(stg_c[:, par * 512:(par + 1) * 512], bank[:, 0:512]))
                    k.waitabs("sp", SGC, sg)
                    STCs[par].inc(sp.dma_start(
                        out=bass.AP(out.tensor, r_cst, [[DM, 128], [1, 512]]),
                        in_=stg_c[:, par * 512:(par + 1) * 512]), 16)
                    sp.reg_add(r_cst, r_cst, 128 * DM)
                sp.reg_add(r_cst, r_cst, 512 - TCN * 128 * DM)

            k.forloop(OBN, ["pe", "sc", "sp"], body, extra={LV: 16})

        def router():
            for tc in range(TCN):
                k.wait("pe", DVm, 0)
                k.wait("pe", SCm, 0)
                for c in range(CI):
                    inst = pe.matmul(P2[:, 0:8],
                                     xn_fm[:, c * SEG + tc * 128:c * SEG + (tc + 1) * 128],
                                     gw_sb[:, c * 8:(c + 1) * 8],
                                     start=(c == 0), stop=(c == CI - 1))
                p1 = PEm.inc(inst)
                k.waitabs("dv", PEm, p1)
                d1 = DVm.inc(dv.tensor_reduce(colA[:], P2[:, 0:8], AX.X, ALU.max, negate=True))
                k.waitabs("sc", DVm, d1)
                s1 = SCm.inc(sc.activation(r_s[:], P2[:, 0:8], AF.Exp, bias=colA[:]))
                k.waitabs("dv", SCm, s1)
                DVm.inc(dv.tensor_reduce(colB[:], r_s[:], AX.X, ALU.add))
                k.wait("dv", DVm, 0)
                DVm.inc(dv.reciprocal(colB[:], colB[:]))
                k.wait("dv", DVm, 0)
                DVm.inc(dv.tensor_scalar_mul(r_s[:], r_s[:], colB[:]))
                k.wait("dv", DVm, 0)
                DVm.inc(dv.tensor_reduce(colA[:], r_s[:], AX.X, ALU.max))
                k.wait("dv", DVm, 0)
                DVm.inc(dv.tensor_scalar(r_s2[:], r_s[:], colA[:], None, op0=ALU.is_ge))
                k.wait("dv", DVm, 0)
                DVm.inc(dv.scalar_tensor_tensor(r_s2[:], r_s2[:], -1e30, r_s[:],
                                                op0=ALU.mult, op1=ALU.add))
                k.wait("dv", DVm, 0)
                DVm.inc(dv.tensor_reduce(colB[:], r_s2[:], AX.X, ALU.max))
                k.wait("dv", DVm, 0)
                DVm.inc(dv.tensor_scalar(r_s2[:], r_s[:], colB[:], None, op0=ALU.is_ge))
                k.wait("dv", DVm, 0)
                DVm.inc(dv.tensor_tensor(r_s2[:], r_s2[:], r_s[:], ALU.mult))
                k.wait("dv", DVm, 0)
                DVm.inc(dv.tensor_reduce(colC[:], r_s2[:], AX.X, ALU.add))
                k.wait("dv", DVm, 0)
                DVm.inc(dv.tensor_scalar_add(colC[:], colC[:], 1e-20))
                k.wait("dv", DVm, 0)
                DVm.inc(dv.reciprocal(colC[:], colC[:]))
                k.wait("dv", DVm, 0)
                DVm.inc(dv.tensor_scalar_mul(cwv[:, 0:8, tc:tc + 1], r_s2[:], colC[:]))

        def phaseD(lq_base, ld_base, spd_base):
            def body():
                for eh in (0, 1):
                    k.wait("dv", DVm, 0)
                    cwsnap = dv.snap(r_cw, min_val=0, max_val=NE * TCN - 8)
                    DVm.inc(dv.tensor_copy(cwcur[:, eh * 8:(eh + 1) * 8], cw_sb[:, ds(cwsnap, 8)]))
                    dv.reg_add(r_cw, r_cw, 8)
                    for fc in range(FCN):
                        slot = fc % 2
                        gbank = P0 if slot == 0 else P1
                        ubank = P2 if slot == 0 else P3
                        for tb in range(TBN):
                            k.waitabs("pe", LQs[fc % 2],
                                      lq_base[fc % 2] + 16 * (8 + eh * 6 + fc // 2 + 1))
                            k.wait("pe", SCm, 0)
                            k.wait("pe", DVm, 0)
                            if fc == 0 and tb == 0:
                                k.wait("pe", SGC, 0)
                            for c in range(CI):
                                ig = pe.matmul(gbank[:, 0:512],
                                               wqg[:, slot * 4352 + c * 128:slot * 4352 + (c + 1) * 128],
                                               xn_fm[:, c * SEG + tb * 512:c * SEG + (tb + 1) * 512],
                                               start=(c == 0), stop=(c == CI - 1))
                            pg = PEm.inc(ig)
                            for c in range(CI):
                                iu = pe.matmul(ubank[:, 0:512],
                                               wqg[:, slot * 4352 + 2048 + c * 128:
                                                   slot * 4352 + 2048 + (c + 1) * 128],
                                               xn_fm[:, c * SEG + tb * 512:c * SEG + (tb + 1) * 512],
                                               start=(c == 0), stop=(c == CI - 1))
                            if tb == TBN - 1:
                                pu = ("FWQ", FWQ.inc(iu))
                            else:
                                pu = ("PEn", PEn.inc(iu))
                            k.waitabs("sc", PEm, pg)
                            k.wait("sc", DVm, 0)
                            s1 = SCm.inc(sc.activation(sg_sb[:, 0:512],
                                                       gbank[:, 0:512], AF.Silu))
                            k.waitabs("dv", SCm, s1)
                            k.waitabs("dv", FWQ if pu[0] == "FWQ" else PEn, pu[1])
                            if fc == 0 and tb == 0:
                                k.wait("dv", FWD, 0)
                            DVm.inc(dv.tensor_tensor(
                                h_fm[:, fc * SEG + tb * 512:fc * SEG + (tb + 1) * 512],
                                sg_sb[:, 0:512], ubank[:, 0:512], ALU.mult))
                    for obp in (0, 1):
                        for tc in range(TCN):
                            gidx = eh * 16 + obp * 8 + tc
                            slot4 = tc % 4
                            for oh in (0, 1):
                                ob = obp * 2 + oh
                                bank = P0 if oh == 0 else P1
                                k.waitabs("pe", LDs[ob % 2],
                                          ld_base[ob % 2] + 16 * (eh * 2 + ob // 2 + 1))
                                k.wait("pe", DVm, 0)
                                k.wait("pe", SGD, 0)
                                if obp == 0 and tc == 0 and oh == 0:
                                    k.waitabs("pe", LQ1,
                                              lq_base[1] + 16 * (8 + eh * 6 + 6))
                                first_dn = None
                                for fc in range(FCN):
                                    idn = pe.matmul(bank[:, 0:512],
                                                    h_fm[:, fc * SEG + tc * 128:fc * SEG + (tc + 1) * 128],
                                                    wdb[:, (ob % 2) * FCN * 512 + fc * 512:
                                                        (ob % 2) * FCN * 512 + (fc + 1) * 512],
                                                    start=(fc == 0), stop=(fc == FCN - 1))
                                    if fc == 0:
                                        first_dn = idn
                                if obp == 0 and tc == 0 and oh == 0:
                                    FWQ.inc(first_dn)
                                if tc == TCN - 1:
                                    fd = FWD.inc(idn)
                                    k.waitabs("dv", FWD, fd)
                                else:
                                    pd = PEm.inc(idn)
                                    k.waitabs("dv", PEm, pd)
                                k.waitabs("dv", SPD, spd_base + 16 * (gidx // 4))
                                inst = dv.tensor_scalar_mul(
                                    stg_d[:, slot4 * 1024 + oh * 512:slot4 * 1024 + (oh + 1) * 512],
                                    bank[:, 0:512],
                                    cwcur[:, eh * 8 + tc:eh * 8 + tc + 1])
                                if oh == 1:
                                    SGD.inc(inst)
                                else:
                                    DVm.inc(inst)

            k.forloop(NE // 2, ["pe", "dv", "sc"], body,
                      extra={LQ0: 192, LQ1: 192, LD0: 64, LD1: 64, SPD: 2 * 4 * 16})
            if not _DRY[0]:
                DV_.reg_add(r_cw, r_cw, -NE * TCN)

        def sp_dstores(sgd_base):
            SROW = 512 * DM

            def tcq_body():
                k.waitabs("sp", SGD, sgd_base + 4)
                apo = bass.AP(dscr.tensor, r_sd,
                              [[DM, 128], [128 * DM, 4], [512, 2], [1, 512]])
                SPD.inc(sp.dma_start(out=apo, in_=stg_d[:, 0:4096]), 16)
                k.wait("sp", SPD, 0)
                sp.reg_add(r_sd, r_sd, SROW)

            def obp_body():
                k.forloop(2, ["sp"], tcq_body, extra={SGD: 4})
                sp.reg_add(r_sd, r_sd, 1024 - 2 * SROW)

            def e_body():
                k.forloop(2, ["sp"], obp_body, extra={SGD: 8})
                sp.reg_add(r_sd, r_sd, 2 * SROW - 2048)

            k.forloop(NE, ["sp"], e_body, extra={SGD: 16})
            if not _DRY[0]:
                SP_.reg_add(r_sd, r_sd, -NE * 1024 * DM)

        def pool_acc(spd_base):
            k.wait("gp", STC0, 0)
            k.wait("gp", STC1, 0)
            if not _DRY[0]:
                GP_.reg_mov(r_uvo, 0)

            def e_body():
                k.waitabs("gp", SPD, spd_base + 64)
                api = bass.AP(dscr.tensor, r_uvo, [[2 * DM, 512], [1, DM]])
                apo = bass.AP(out.tensor, r_so, [[2 * DM, 512], [1, DM]])
                STD.inc(gp.dma_start(out=apo, in_=api, accum_op=ALU.add), 16)
                gp.reg_add(r_uvo, r_uvo, DM)
                gp.reg_add(r_so, r_so, DM)
                api2 = bass.AP(dscr.tensor, r_uvo, [[2 * DM, 512], [1, DM]])
                apo2 = bass.AP(out.tensor, r_so, [[2 * DM, 512], [1, DM]])
                STD.inc(gp.dma_start(out=apo2, in_=api2, accum_op=ALU.add), 16)
                gp.reg_add(r_so, r_so, -DM)
                k.wait("gp", STD, 0)
                gp.reg_add(r_uvo, r_uvo, 1024 * DM - DM)

            k.forloop(NE, ["gp"], e_body, extra={SPD: 64})
            if not _DRY[0]:
                GP_.reg_add(r_so, r_so, SEG * DM)

        gp.wait_ge(LC.h, LC.n)
        k.barrier()

        def prebody():
            lv0 = LV.n
            lq0 = [LQ0.n, LQ1.n]
            if not _DRY[0]:
                GP_.reg_mov(r_uvo, 0)
                GP_.reg_mov(r_uqg, 0)
            poolB_stream()
            phaseA(x_pre)
            phaseB(True, lv0, lq0)
            k.barrier()

        if "P" in parts:
            k.forloop(NPRE, ["pe", "dv", "sc", "gp", "sp"], prebody)

        SP_.reg_mov(r_xrow, 0)

        def segbody():
            lv0 = LV.n
            lq0 = [LQ0.n, LQ1.n]
            ld0 = [LD0.n, LD1.n]
            spd0 = SPD.n
            sgd0 = SGD.n
            if not _DRY[0]:
                GP_.reg_mov(r_uvo, 0)
                GP_.reg_mov(r_uqg, 0)
                GP_.reg_mov(r_uwd, 0)
                DV_.reg_mov(r_cw, 0)
            if "B" in parts:
                poolB_stream()
            if "C" in parts:
                pool_stream(4, wvo_th, CI * 512, r_uvo, [LV], FWV, [wvb[:]])

            def poolD():
                pool_stream(6, wpr_th, 4352, r_uqg, [LQ0, LQ1], FWQ,
                            [wqg[:, 0:4352], wqg[:, 4352:8704]])
                pool_stream(2, wd_th, FCN * 512, r_uwd, [LD0, LD1], FWD,
                            [wdb[:, 0:FCN * 512], wdb[:, FCN * 512:2 * FCN * 512]])

            if "D" in parts:
                k.forloop(NE, ["gp"], poolD, extra={FWQ: 12, FWD: 4})
            if "A" in parts:
                phaseA(x_own)
            if "B" in parts:
                phaseB(False, lv0, lq0)
            if "C" in parts:
                phaseC(lv0)
            if "R" in parts:
                router()
            if "D" in parts:
                phaseD(lq0, ld0, spd0)
            if "S" in parts:
                sp_dstores(sgd0)
                pool_acc(spd0)
            if "C" in parts and not _DRY[0]:
                SP_.reg_add(r_cst, r_cst, SEG * DM - 4 * 512)
            k.barrier()

        k.forloop(NSEG, ["pe", "dv", "sc", "gp", "sp"], segbody)

        if "C" in parts:
            gp.wait_ge(STC0.h, STC0.n)
            gp.wait_ge(STC1.h, STC1.n)
        for off in (0, DM):
            apo = bass.AP(out.tensor, off, [[2 * DM, TOK // 2], [1, DM]])
            api = bass.AP(x_own.tensor, off, [[2 * DM, TOK // 2], [1, DM]])
            STD.inc(gp.dma_start(out=apo, in_=api, accum_op=ALU.add), 16)
        gp.wait_ge(STD.h, STD.n)
    return nc



def _stage_weights(inp):
    f32 = np.float32
    asn = lambda a: np.ascontiguousarray(a, dtype=f32)

    Wq, Wk = np.asarray(inp["L0_attn_Wq"]), np.asarray(inp["L0_attn_Wk"])
    Wv, Wo = np.asarray(inp["L0_attn_Wv"]), np.asarray(inp["L0_attn_Wo"])
    Wfq, Wfk = np.asarray(inp["L0_attn_Wfq"]), np.asarray(inp["L0_attn_Wfk"])
    gwt = np.asarray(inp["L0_ffn_gate_weight"])
    wg, wu, wd = (np.asarray(inp["L0_ffn_w_gate"]), np.asarray(inp["L0_ffn_w_up"]),
                  np.asarray(inp["L0_ffn_w_down"]))
    sg, su, sd = (np.asarray(inp["L0_ffn_sw_gate"]), np.asarray(inp["L0_ffn_sw_up"]),
                  np.asarray(inp["L0_ffn_sw_down"]))

    def hp_tiles(W):
        return W.T.reshape(CI, 128, 16, 128).transpose(2, 1, 0, 3)

    wqk = np.stack([hp_tiles(Wq), hp_tiles(Wk)], axis=2).reshape(16, 128, 2 * CI * 128)
    wftail = np.zeros((16, 128, 256), dtype=f32)
    for hp in range(16):
        blk = np.zeros((64, 256), dtype=f32)
        for hloc in range(2):
            blk[:, hloc * 64:(hloc + 1) * 64] = Wfq[hp * 2 + hloc].astype(f32)
            blk[:, 128 + hloc * 64:128 + (hloc + 1) * 64] = Wfk[hp * 2 + hloc].astype(f32)
        wftail[hp, 0:64] = blk
        wftail[hp, 64:128] = blk
    wqk = np.concatenate([wqk, wftail], axis=2)

    def gu_tiles(g, u):
        gT = g.T.reshape(CI, 128, FCN, 128).transpose(2, 1, 0, 3)
        uT = u.T.reshape(CI, 128, FCN, 128).transpose(2, 1, 0, 3)
        return np.stack([gT, uT], axis=2).reshape(FCN, 128, 2 * CI * 128)

    wgu_list = [gu_tiles(wg[e], wu[e]) for e in range(8)]
    wgu_list.append(gu_tiles(sg[0:1408], su[0:1408]))
    wgu_list.append(gu_tiles(sg[1408:2816], su[1408:2816]))
    wgu_list = [np.concatenate([g, g[0:1]], axis=0) for g in wgu_list]
    wgu_all = np.concatenate(wgu_list, axis=0)
    wgu_all = np.concatenate([wgu_all, np.zeros((120, 128, 256), dtype=f32)], axis=2)
    wpr_t = asn(np.concatenate([wqk, wgu_all], axis=0))

    def big_tiles(W):
        return W.T.reshape(CI, 128, 4, 512).transpose(2, 1, 0, 3).reshape(4, 128, CI * 512)

    wvo_t = asn(np.concatenate([big_tiles(Wv), big_tiles(Wo)], axis=0))

    def d_tiles(W):
        return W.T.reshape(FCN, 128, OBN, 512).transpose(2, 1, 0, 3).reshape(OBN, 128, FCN * 512)

    wd_list = [d_tiles(wd[e]) for e in range(8)]
    wd_list.append(d_tiles(sd[:, 0:1408]))
    wd_list.append(d_tiles(sd[:, 1408:2816]))
    wd_t = asn(np.concatenate(wd_list, axis=0))

    gw_t = asn(gwt.T.reshape(CI, 128, 8).transpose(1, 0, 2).reshape(128, CI * 8))
    ident = np.eye(128, dtype=ml_dtypes.bfloat16)
    maskut = np.triu(np.ones((128, 128))).astype(ml_dtypes.bfloat16)
    cva = np.tile(np.asarray([CONST_VALS], dtype=np.float32), (128, 1))

    return dict(wvo_t=wvo_t, wpr_t=wpr_t, wd_t=wd_t, gw_t=gw_t,
                ident=ident, maskut=maskut, cvals=cva)


_NC_CACHE = [None]


def kernel(**inputs):
    x = np.ascontiguousarray(np.asarray(inputs["L0_attn_hidden_states"], dtype=np.float32))
    w = _stage_weights(inputs)
    if _NC_CACHE[0] is None:
        _NC_CACHE[0] = build_nc()
    nc = _NC_CACHE[0]
    in_maps = []
    for core in range(8):
        b, half = core // 2, core % 2
        m = dict(w)
        m["x_own"] = np.ascontiguousarray(x[b, half * TOK:(half + 1) * TOK, :])
        m["x_pre"] = np.ascontiguousarray(x[b, 0:TOK, :])
        m["flagd"] = np.full((128, 1), float(half), dtype=np.float32)
        in_maps.append(m)
    res = run_bass_kernel_spmd(nc, in_maps, core_ids=list(range(8)))
    outp = np.empty((B, Lq, DM), dtype=np.float32)
    for core in range(8):
        b, half = core // 2, core % 2
        outp[b, half * TOK:(half + 1) * TOK, :] = np.asarray(res.results[core]["out"])
    return outp


# version 04
# speedup vs baseline: 1.1974x; 1.1974x over previous
import numpy as np
import ml_dtypes
import contextlib
import concourse.bass as bass
import concourse.mybir as mybir
from concourse.bass import ds
from concourse.bass_utils import run_bass_kernel_spmd
from concourse.alu_op_type import AluOpType as ALU

F32 = mybir.dt.float32
BF16 = mybir.dt.bfloat16
AF = mybir.ActivationFunctionType
AX = mybir.AxisListType
ET = mybir.EngineType

B, Lq, DM = 4, 4096, 2048
TOK, SEG = 2048, 1024
NSEG, NPRE = 2, 2
TCN, TBN, CI = 8, 2, 16
H, HD, FD = 32, 64, 64
NE, FCN, OBN = 10, 11, 4
EPS_LN, EPS_AT = 1e-5, 1e-12
CONST_VALS = [EPS_LN, -0.5, 1.0 / DM, -1.0, EPS_AT, -1e30, 1e-20, 2.0]

ENAMES = {"pe": ET.PE, "dv": ET.DVE, "sc": ET.Activation, "gp": ET.Pool, "sp": ET.SP}
_DRY = [False]


class _MockInst:
    def then_inc(self, *a, **k):
        return self


_MOCK = _MockInst()


class Eng:
    def __init__(self, real):
        self._real = real

    def __getattr__(self, name):
        if _DRY[0]:
            if name == "snap":
                return lambda *a, **k: 0
            return lambda *a, **k: _MOCK
        return getattr(self._real, name)


class S:
    def __init__(self, nc, es, name):
        self.h = es.enter_context(nc.semaphore(name))
        self.n = 0
        self.name = name

    def inc(self, inst, k=1):
        self.n += k
        if inst is not None and not isinstance(inst, _MockInst):
            inst.then_inc(self.h, k)
        return self.n


class KB:
    def __init__(self, nc, es):
        self.nc = nc
        self.es = es
        self.sems = []
        attr = {"pe": "tensor", "dv": "vector", "sc": "scalar", "gp": "gpsimd", "sp": "sync"}
        self.eng = {n: Eng(getattr(nc, a)) for n, a in attr.items()}
        self.real = {n: getattr(nc, a) for n, a in attr.items()}
        self.rW = {}
        self.tmp = {}

    def sem(self, name):
        s = S(self.nc, self.es, name)
        self.sems.append(s)
        return s

    def declare_waiters(self, pairs):
        for en, sem in pairs:
            r = self.real[en].alloc_register(f"rW_{en}_{sem.name}")
            self.real[en].reg_mov(r, 0)
            self.rW[(en, sem)] = r
        for en in ENAMES:
            t = self.real[en].alloc_register(f"tmp_{en}")
            self.tmp[en] = t

    def wait(self, en, sem, off=0):
        self.waitabs(en, sem, sem.n + off)

    def waitabs(self, en, sem, static):
        if _DRY[0]:
            return
        e = self.real[en]
        key = (en, sem)
        if key in self.rW:
            e.reg_alu(self.tmp[en], self.rW[key], static, mybir.AluOpType.add)
            e.wait_ge(sem.h, self.tmp[en])
        else:
            assert static >= 0
            e.wait_ge(sem.h, static)

    def forloop(self, n, engines, body, extra=None):
        extra = extra or {}
        snap = [s.n for s in self.sems]
        if _DRY[0]:
            body()
            for s, s0 in zip(self.sems, snap):
                s.n = s0 + n * (s.n - s0)
            return
        _DRY[0] = True
        body()
        real_d = {s: s.n - s0 for s, s0 in zip(self.sems, snap)}
        for s, s0 in zip(self.sems, snap):
            s.n = s0
        _DRY[0] = False
        deltas = dict(real_d)
        for s, d in extra.items():
            deltas[s] = deltas.get(s, 0) + d

        nc = self.nc
        engs = [ENAMES[e] for e in engines]
        loop_id = nc.next_id()
        ls, le = f"kf_{loop_id}_loop", f"kf_{loop_id}_end"
        registers = nc.alloc_registers(f"kf_{loop_id}_i", engines=engs)
        nc.regs_mov(registers, 0)
        nc.br(ls, engines=engs)
        with nc.body(ls, valid_engines=engs):
            body()
            for en in engines:
                for (wen, sem), r in self.rW.items():
                    if wen == en and deltas.get(sem, 0):
                        self.real[en].reg_alu(r, r, deltas[sem], mybir.AluOpType.add)
            nc.regs_alu(registers, registers, 1, op=mybir.AluOpType.add)
            nc.br_lt(registers, n, on_true=ls, on_false=le, engines=engs)
        nc.switch_bb(le)
        for e, h in zip(registers.engines, registers.handles):
            nc.engines[e].free_register(h)
        for en in engines:
            for (wen, sem), r in self.rW.items():
                if wen == en and deltas.get(sem, 0):
                    self.real[en].reg_alu(r, r, -n * deltas[sem], mybir.AluOpType.add)
        for s, s0 in zip(self.sems, snap):
            s.n = s0 + n * real_d.get(s, 0)

    def barrier(self):
        if not _DRY[0]:
            self.nc.all_engine_barrier()


def build_nc(parts="PABCRDS"):
    nc = bass.Bass("TRN2", target_bir_lowering=False, debug=False)
    dt = nc.dram_tensor
    x_own = dt("x_own", [TOK, DM], F32, kind="ExternalInput").ap()
    x_pre = dt("x_pre", [TOK, DM], F32, kind="ExternalInput").ap()
    flagd = dt("flagd", [128, 1], F32, kind="ExternalInput").ap()
    wvo_t = dt("wvo_t", [8, 128, CI * 512], F32, kind="ExternalInput").ap()
    wpr_t = dt("wpr_t", [16 + NE * 12, 128, 4352], F32, kind="ExternalInput").ap()
    wd_t = dt("wd_t", [NE * OBN, 128, FCN * 512], F32, kind="ExternalInput").ap()
    gw_t = dt("gw_t", [128, CI * 8], F32, kind="ExternalInput").ap()
    ident = dt("ident", [128, 128], BF16, kind="ExternalInput").ap()
    maskut = dt("maskut", [128, 256], BF16, kind="ExternalInput").ap()
    cvals = dt("cvals", [128, 8], F32, kind="ExternalInput").ap()
    out = dt("out", [TOK, DM], F32, kind="ExternalOutput").ap()
    dscr = dt("dscr", [NE * 1024, DM], F32, kind="Internal").ap()

    es = contextlib.ExitStack()
    sbt = lambda name, shape, dty: es.enter_context(nc.sbuf_tensor(name, shape, dty))
    pst = lambda name, shape, dty: es.enter_context(nc.psum_tensor(name, shape, dty))

    xn_fm = sbt("xn_fm", [128, CI * SEG], BF16)
    yat = sbt("yat", [128, 16 * SEG], BF16)
    h_fm = sbt("h_fm", [128, FCN * SEG], BF16)
    v_ext = sbt("v_ext", [128, 8 * TCN * 65], BF16)
    qk_fm = sbt("qk_fm", [128, 2 * SEG], BF16)
    x_sb = sbt("x_sb", [128, DM], F32)
    wvb = sbt("wvb", [128, CI * 512], BF16)
    wqg = sbt("wqg", [128, 2 * 4352], BF16)
    wdb = sbt("wdb", [128, 2 * FCN * 512], BF16)
    stg_c = sbt("stg_c", [128, 2 * 512], F32)
    stg_d = sbt("stg_d", [128, 4 * 1024], F32)
    gw_sb = sbt("gw_sb", [128, CI * 8], BF16)
    cw_sb = sbt("cw_sb", [128, NE * TCN], F32)
    cwcur = sbt("cwcur", [128, 2 * TCN], F32)
    S32 = sbt("S32", [64, H * 65], F32)
    S_cur = sbt("S_cur", [64, 8 * 65], F32)
    S16c = sbt("S16c", [64, 130], BF16)
    y_cur = sbt("y_cur", [128, 4 * SEG], BF16)
    sg_sb = sbt("sg_sb", [128, 512], F32)
    id_sb = sbt("id_sb", [128, 128], BF16)
    mk2 = sbt("mk2", [128, 256], BF16)
    ones64 = sbt("ones64", [1, 64], F32)
    flag_sb = sbt("flag_sb", [128, 1], F32)
    cv_sb = sbt("cv_sb", [128, 8], F32)
    ex_k = sbt("ex_k", [128, 128], BF16)
    fq_tm = sbt("fq_tm", [128, 128], BF16)
    fk_tm = sbt("fk_tm", [128, 128], BF16)
    fqkT = sbt("fqkT", [64, 512], BF16)
    atm = sbt("atm", [128, 256], BF16)
    rd_f = sbt("rd_f", [1, 256], F32)
    rden = sbt("rden", [64, 256], F32)
    colA = sbt("colA", [128, 1], F32)
    colB = sbt("colB", [128, 1], F32)
    colC = sbt("colC", [128, 1], F32)
    colD = sbt("colD", [128, 1], F32)
    r_s = sbt("r_s", [128, 8], F32)
    r_s2 = sbt("r_s2", [128, 8], F32)

    for i, v in enumerate(CONST_VALS):
        nc.const_aps.register(F32, v, cv_sb[:, i:i + 1])

    P0 = pst("P0", [128, 512], F32)
    P1 = pst("P1", [128, 512], F32)
    P2 = pst("P2", [128, 512], F32)
    P3 = pst("P3", [128, 512], F32)
    P4 = pst("P4", [128, 512], F32)
    P5 = pst("P5", [128, 512], F32)
    P6 = pst("P6", [64, 512], F32)
    P7 = pst("P7", [128, 1024], BF16)

    k = KB(nc, es)
    LV = k.sem("LV")
    LQ0 = k.sem("LQ0")
    LQ1 = k.sem("LQ1")
    LQs = [LQ0, LQ1]
    LD0 = k.sem("LD0")
    LD1 = k.sem("LD1")
    LDs = [LD0, LD1]
    LXA = k.sem("LXA")
    FXA = k.sem("FXA")
    FWV = k.sem("FWV")
    FWQ = k.sem("FWQ")
    FWD = k.sem("FWD")
    PEm = k.sem("PEm")
    PEn = k.sem("PEn")
    DVm = k.sem("DVm")
    SCm = k.sem("SCm")
    STC0 = k.sem("STC0")
    STC1 = k.sem("STC1")
    STCs = [STC0, STC1]
    STD = k.sem("STD")
    SPD = k.sem("SPD")
    SGC = k.sem("SGC")
    SGD = k.sem("SGD")
    LC = k.sem("LC")

    k.declare_waiters([
        ("pe", LV), ("pe", LQ0), ("pe", LQ1), ("pe", LD0), ("pe", LD1),
        ("pe", DVm), ("pe", SCm), ("pe", FXA), ("pe", SGC), ("pe", SGD),
        ("dv", PEm), ("dv", PEn), ("dv", SCm), ("dv", LXA), ("dv", SPD), ("dv", FWD), ("dv", DVm),
        ("dv", FWV), ("dv", FWQ),
        ("sc", DVm), ("sc", PEm), ("sc", LXA), ("sc", STC0), ("sc", STC1), ("sc", SCm), ("sc", FWV),
        ("gp", FWV), ("gp", FWQ), ("gp", FWD), ("gp", SPD), ("gp", STD), ("gp", STC0), ("gp", STC1),
        ("sp", FXA), ("sp", SGC), ("sp", SGD), ("sp", SPD),
    ])
    pe, dv, sc, gp, sp = (k.eng[n] for n in ("pe", "dv", "sc", "gp", "sp"))
    PE_, DV_, SC_, GP_, SP_ = (k.real[n] for n in ("pe", "dv", "sc", "gp", "sp"))

    r_uvo = GP_.alloc_register("r_uvo")
    r_uqg = GP_.alloc_register("r_uqg")
    r_uwd = GP_.alloc_register("r_uwd")
    r_so = GP_.alloc_register("r_so")
    r_xrow = SP_.alloc_register("r_xrow")
    r_cst = SP_.alloc_register("r_cst")
    r_sd = SP_.alloc_register("r_sd")
    r_tcA = DV_.alloc_register("r_tcA")
    r_cw = DV_.alloc_register("r_cw")
    r_g8s = SC_.alloc_register("r_g8s")
    r_g8y = SC_.alloc_register("r_g8y")

    wvo_th = wvo_t.tensor
    wpr_th = wpr_t.tensor
    wd_th = wd_t.tensor
    xn4 = xn_fm[:].rearrange("p (c t) -> p c t", c=CI)
    p7v = P7[:].rearrange("p (j e) -> p j e", j=8)
    vve = v_ext[:].rearrange("p (a b c) -> p a b c", a=8, b=TCN, c=65)
    cwv = cw_sb[:].rearrange("p (a b) -> p a b", a=NE, b=TCN)

    with es:
        LC.inc(gp.dma_start(out=cv_sb[:], in_=cvals), 16)
        LC.inc(gp.dma_start(out=id_sb[:], in_=ident), 16)
        LC.inc(gp.dma_start(out=mk2[:], in_=maskut), 16)
        LC.inc(gp.dma_start(out=gw_sb[:], in_=gw_t), 16)
        LC.inc(gp.dma_start(out=flag_sb[:], in_=flagd), 16)
        sp.sem_inc(FWV.h, 1)
        FWV.n += 1
        sp.sem_inc(FWQ.h, 2)
        FWQ.n += 2
        sp.sem_inc(FWD.h, 2)
        FWD.n += 2

        sp.sem_inc(SGC.h, 2)
        SGC.n += 2
        dv.memset(S32[:], 0.0)
        dv.memset(vve[:, :, :, 64:65], 1.0)
        dv.memset(ones64[:], 1.0)
        dv.memset(cwv[:, 8:10, :], 1.0)
        for r in (r_uvo, r_uqg, r_uwd, r_so):
            gp.reg_mov(r, 0)
        for r in (r_xrow, r_cst, r_sd):
            sp.reg_mov(r, 0)
        for r in (r_tcA, r_cw):
            dv.reg_mov(r, 0)
        for r in (r_g8s, r_g8y):
            sc.reg_mov(r, 0)
        k.barrier()

        def pool_stream(n, src_th, elen, reg, sem_loads, sem_free, dests):

            def body():
                u0 = sum(s.n for s in sem_loads) // 16
                for j, (dst, sl) in enumerate(zip(dests, sem_loads)):
                    k.waitabs("gp", sem_free, u0 + j + 1)
                    ap = bass.AP(src_th, reg, [[elen, 128], [1, elen]])
                    sl.inc(gp.dma_start(out=dst, in_=ap), 16)
                    gp.reg_add(reg, reg, 128 * elen)

            k.forloop(n, ["gp"], body, extra={sem_free: len(dests)})

        def poolB_stream():

            def body():
                uv0 = LV.n // 16
                k.waitabs("gp", FWV, uv0 + 1)
                apv = bass.AP(wvo_th, r_uvo, [[CI * 512, 128], [1, CI * 512]])
                LV.inc(gp.dma_start(out=wvb[:], in_=apv), 16)
                gp.reg_add(r_uvo, r_uvo, 128 * CI * 512)
                uq0 = (LQ0.n + LQ1.n) // 16

                def qpair():
                    for j in (0, 1):
                        k.waitabs("gp", FWQ, uq0 + j + 1)
                        apq = bass.AP(wpr_th, r_uqg, [[4352, 128], [1, 4352]])
                        LQs[j].inc(gp.dma_start(out=wqg[:, j * 4352:(j + 1) * 4352], in_=apq), 16)
                        gp.reg_add(r_uqg, r_uqg, 128 * 4352)

                k.forloop(2, ["gp"], qpair, extra={FWQ: 2})

            k.forloop(4, ["gp"], body, extra={FWV: 1, FWQ: 4})


        def phaseA(xsrc):
            def body():
                k.wait("sp", FXA, 0)
                rr = sp.snap(r_xrow, min_val=0, max_val=TOK - 128)
                lx = LXA.inc(sp.dma_start(out=x_sb[:], in_=xsrc[ds(rr, 128), :]), 16)
                sp.reg_add(r_xrow, r_xrow, 128)
                k.waitabs("sc", LXA, lx)
                sq = SCm.inc(sc.activation(h_fm[:, 0:2048], x_sb[:], AF.Square, accum_out=colA[:]))
                k.waitabs("dv", LXA, lx)
                DVm.inc(dv.tensor_reduce(colB[:], x_sb[:], AX.X, ALU.add))
                k.wait("dv", DVm, 0)
                DVm.inc(dv.tensor_scalar_mul(colB[:], colB[:], 1.0 / DM))
                k.wait("dv", DVm, 0)
                k.waitabs("dv", SCm, sq)
                dv.tensor_scalar_mul(colC[:], colA[:], 1.0 / DM)
                DVm.inc(dv.tensor_tensor(colD[:], colB[:], colB[:], ALU.mult))
                k.wait("dv", DVm, 0)
                d3 = DVm.inc(dv.tensor_tensor(colC[:], colC[:], colD[:], ALU.subtract))
                k.waitabs("sc", DVm, d3)
                s2 = SCm.inc(sc.activation(colC[:], colC[:], AF.Ln, bias=EPS_LN))
                k.waitabs("sc", SCm, s2)
                s3 = SCm.inc(sc.activation(colC[:], colC[:], AF.Exp, scale=-0.5))
                k.waitabs("dv", SCm, s3)
                d4 = DVm.inc(dv.tensor_scalar(colD[:], colB[:], colC[:], -1.0,
                                              op0=ALU.mult, op1=ALU.mult))
                k.waitabs("sc", DVm, d4)
                k.wait("sc", PEm, 0)
                fx = FXA.inc(sc.activation(h_fm[:, 2048:4096], x_sb[:], AF.Identity,
                                           bias=colD[:], scale=colC[:]))
                toff = dv.snap(r_tcA, min_val=0, max_val=SEG - 128)
                for half in (0, 1):
                    k.waitabs("pe", FXA, fx)
                    k.wait("pe", DVm, 0)
                    inst = None
                    for j in range(8):
                        c = half * 8 + j
                        inst = pe.transpose(P7[:, j * 128:(j + 1) * 128],
                                            h_fm[:, 2048 + c * 128:2048 + (c + 1) * 128],
                                            id_sb[:])
                    p1 = PEm.inc(inst)
                    k.waitabs("dv", PEm, p1)
                    DVm.inc(dv.tensor_copy(xn4[:, half * 8:(half + 1) * 8, ds(toff, 128)],
                                           p7v[:, :, :]))
                dv.reg_add(r_tcA, r_tcA, 128)

            k.forloop(TCN, ["pe", "dv", "sc", "sp"], body)
            if not _DRY[0]:
                DV_.reg_add(r_tcA, r_tcA, -SEG)

        def vproj(lv_abs):
            for tc in range(TCN):
                bank = P0 if tc % 2 == 0 else P1
                bk4 = bank[:].rearrange("p (a b) -> p a b", a=8)
                k.waitabs("pe", LV, lv_abs)
                k.wait("pe", DVm, 0)
                for c in range(CI):
                    inst = pe.matmul(bank[:, 0:512],
                                     xn_fm[:, c * SEG + tc * 128:c * SEG + (tc + 1) * 128],
                                     wvb[:, c * 512:(c + 1) * 512],
                                     start=(c == 0), stop=(c == CI - 1))
                if tc == TCN - 1:
                    fv = FWV.inc(inst)
                    k.waitabs("dv", FWV, fv)
                else:
                    p1 = PEm.inc(inst)
                    k.waitabs("dv", PEm, p1)
                DVm.inc(dv.tensor_copy(vve[:, :, tc, 0:64], bk4[:, :, :]))

        def qkproj(lq_abs, slot, prime):
            ms = (1,) if prime else (0, 1)
            for m in ms:
                for tb in range(TBN):
                    bank = P0 if tb == 0 else P1
                    k.waitabs("pe", LQs[slot], lq_abs)
                    k.wait("pe", DVm, 0)
                    for c in range(CI):
                        inst = pe.matmul(bank[:, 0:512],
                                         wqg[:, slot * 4352 + m * 2048 + c * 128:
                                             slot * 4352 + m * 2048 + (c + 1) * 128],
                                         xn_fm[:, c * SEG + tb * 512:c * SEG + (tb + 1) * 512],
                                         start=(c == 0), stop=(c == CI - 1))
                    p1 = PEm.inc(inst)
                    k.waitabs("dv", PEm, p1)
                    DVm.inc(dv.tensor_copy(qk_fm[:, m * SEG + tb * 512:m * SEG + (tb + 1) * 512],
                                           bank[:, 0:512]))

        def mixer_pair(hp, prime):
            hA, hB = 2 * (hp % 4), 2 * (hp % 4) + 1
            slot = hp % 2
            wfq_A = wqg[0:64, slot * 4352 + 4096:slot * 4352 + 4096 + 64]
            wfq_B = wqg[64:128, slot * 4352 + 4160:slot * 4352 + 4160 + 64]
            wfk_A = wqg[0:64, slot * 4352 + 4224:slot * 4352 + 4224 + 64]
            wfk_B = wqg[64:128, slot * 4352 + 4288:slot * 4352 + 4288 + 64]
            sP = S_cur[0:64, hA * 65:hA * 65 + 130]
            if not prime:
                k.wait("dv", SCm, 0)
                k.wait("dv", DVm, 0)
                DVm.inc(dv.tensor_copy(S16c[:], sP))
            for c in range(TCN):
                k.wait("pe", DVm, 0)
                k.wait("pe", SCm, 0)
                if not prime:
                    pe.matmul(P4[:, 0:64], qk_fm[0:64, c * 128:(c + 1) * 128],
                              wfq_A, start=True, stop=True)
                pe.matmul(P4[:, 64:128], qk_fm[0:64, SEG + c * 128:SEG + (c + 1) * 128],
                          wfk_A, start=True, stop=True)
                if not prime:
                    pe.matmul(P3[:, 256:320], qk_fm[64:128, c * 128:(c + 1) * 128],
                              wfq_B, start=True, stop=True)
                p1 = PEm.inc(pe.matmul(P3[:, 320:384],
                                       qk_fm[64:128, SEG + c * 128:SEG + (c + 1) * 128],
                                       wfk_B, start=True, stop=True))
                k.waitabs("dv", PEm, p1)
                if prime:
                    dv.tensor_reduce(r_s[:, 1:2], P4[:, 64:128], AX.X, ALU.max, negate=True)
                    d1 = DVm.inc(dv.tensor_reduce(r_s[:, 3:4], P3[:, 320:384],
                                                  AX.X, ALU.max, negate=True))
                else:
                    dv.tensor_reduce(
                        r_s[:, 0:2], P4[:, 0:128].rearrange("p (a b) -> p a b", b=64),
                        AX.X, ALU.max, negate=True)
                    d1 = DVm.inc(dv.tensor_reduce(
                        r_s[:, 2:4], P3[:, 256:384].rearrange("p (a b) -> p a b", b=64),
                        AX.X, ALU.max, negate=True))
                k.waitabs("sc", DVm, d1)
                if not prime:
                    sc.activation(fq_tm[:, 0:64], P4[:, 0:64], AF.Exp, bias=r_s[:, 0:1])
                    sc.activation(fq_tm[:, 64:128], P3[:, 256:320], AF.Exp, bias=r_s[:, 2:3])
                sc.activation(ex_k[:, 0:64], P4[:, 64:128], AF.Exp, bias=r_s[:, 1:2])
                s1 = SCm.inc(sc.activation(ex_k[:, 64:128], P3[:, 320:384],
                                           AF.Exp, bias=r_s[:, 3:4]))
                k.waitabs("dv", SCm, s1)
                DVm.inc(dv.tensor_reduce(
                    r_s2[:, 0:2], ex_k[:].rearrange("p (a b) -> p a b", b=64),
                    AX.X, ALU.add))
                k.wait("dv", DVm, 0)
                DVm.inc(dv.reciprocal(r_s2[:, 0:2], r_s2[:, 0:2]))
                k.wait("dv", DVm, 0)
                if prime:
                    dv.tensor_scalar(fk_tm[:, 0:64], ex_k[:, 0:64], r_s2[:, 0:1],
                                     flag_sb[:], op0=ALU.mult, op1=ALU.mult)
                    dfk = DVm.inc(dv.tensor_scalar(fk_tm[:, 64:128], ex_k[:, 64:128],
                                                   r_s2[:, 1:2], flag_sb[:],
                                                   op0=ALU.mult, op1=ALU.mult))
                else:
                    dv.tensor_scalar_mul(fk_tm[:, 0:64], ex_k[:, 0:64], r_s2[:, 0:1])
                    dfk = DVm.inc(dv.tensor_scalar_mul(fk_tm[:, 64:128], ex_k[:, 64:128],
                                                       r_s2[:, 1:2]))

                if not prime:
                    k.waitabs("pe", DVm, dfk)
                    pe.transpose(P7[0:64, 0:128], fq_tm[:, 0:64], id_sb[:])
                    pe.transpose(P7[0:64, 128:256], fq_tm[:, 64:128], id_sb[:])
                    pe.transpose(P7[0:64, 256:384], fk_tm[:, 0:64], id_sb[:])
                    p2 = PEm.inc(pe.transpose(P7[0:64, 384:512], fk_tm[:, 64:128], id_sb[:]))
                    k.waitabs("sc", PEm, p2)
                    s2 = SCm.inc(sc.copy(fqkT[0:64, 0:512], P7[0:64, 0:512]))
                    k.waitabs("pe", SCm, s2)
                    pe.matmul(P3[:, 0:128], fqkT[0:64, 256:384], fqkT[0:64, 0:128],
                              start=True, stop=True)
                    p3 = PEm.inc(pe.matmul(P3[:, 128:256], fqkT[0:64, 384:512],
                                           fqkT[0:64, 128:256], start=True, stop=True))
                    k.waitabs("dv", PEm, p3)
                    d4 = DVm.inc(dv.tensor_tensor(atm[:, 0:256], P3[:, 0:256],
                                                  mk2[:, 0:256], ALU.mult))
                    k.waitabs("pe", DVm, d4)
                    pe.matmul(P5[0:64, 256:384],
                              v_ext[:, (hA * TCN + c) * 65:(hA * TCN + c) * 65 + 64],
                              atm[:, 0:128], start=True, stop=False)
                    pe.matmul(P5[0:64, 256:384], S16c[0:64, 0:64], fqkT[0:64, 0:128],
                              start=False, stop=True)
                    pe.matmul(P5[0:64, 384:512],
                              v_ext[:, (hB * TCN + c) * 65:(hB * TCN + c) * 65 + 64],
                              atm[:, 128:256], start=True, stop=False)
                    pe.matmul(P5[0:64, 384:512], S16c[0:64, 65:129], fqkT[0:64, 128:256],
                              start=False, stop=True)
                    pe.matmul(P2[0:1, 0:128], mk2[:, 127:128], atm[:, 0:128],
                              start=True, stop=False)
                    pe.matmul(P2[0:1, 0:128], S16c[0:64, 64:65], fqkT[0:64, 0:128],
                              start=False, stop=True)
                    pe.matmul(P2[0:1, 128:256], mk2[:, 127:128], atm[:, 128:256],
                              start=True, stop=False)
                    p4 = PEm.inc(pe.matmul(P2[0:1, 128:256], S16c[0:64, 129:130],
                                           fqkT[0:64, 128:256], start=False, stop=True))
                    k.waitabs("dv", PEm, p4)
                    DVm.inc(dv.tensor_scalar_add(rd_f[:], P2[0:1, 0:256], EPS_AT))
                    k.wait("dv", DVm, 0)
                    d6 = DVm.inc(dv.reciprocal(rd_f[:], rd_f[:]))
                    k.waitabs("pe", DVm, d6)
                    pe.matmul(P2[0:64, 256:384], ones64[:], rd_f[0:1, 0:128],
                              start=True, stop=True)
                    p5 = PEm.inc(pe.matmul(P2[0:64, 384:512], ones64[:], rd_f[0:1, 128:256],
                                           start=True, stop=True))
                    k.waitabs("dv", PEm, p5)
                    DVm.inc(dv.tensor_copy(rden[:, 0:256], P2[0:64, 256:512]))
                    k.wait("dv", DVm, 0)
                    k.wait("dv", SCm, 0)
                    cblk = (hp % 4) * SEG + c * 128
                    dv.tensor_tensor(y_cur[0:64, cblk:cblk + 128],
                                     P5[0:64, 256:384], rden[:, 0:128], ALU.mult)
                    DVm.inc(dv.tensor_tensor(y_cur[64:128, cblk:cblk + 128],
                                             P5[0:64, 384:512], rden[:, 128:256], ALU.mult))
                k.waitabs("pe", DVm, dfk)
                pe.matmul(P6[0:64, 0:65], fk_tm[:, 0:64],
                          v_ext[:, (hA * TCN + c) * 65:(hA * TCN + c + 1) * 65],
                          start=True, stop=True)
                inst6 = pe.matmul(P6[0:64, 65:130], fk_tm[:, 64:128],
                                  v_ext[:, (hB * TCN + c) * 65:(hB * TCN + c + 1) * 65],
                                  start=True, stop=True)
                if c == TCN - 1:
                    f6 = FWQ.inc(inst6)
                    k.waitabs("dv", FWQ, f6)
                else:
                    p6 = PEm.inc(inst6)
                    k.waitabs("dv", PEm, p6)
                k.wait("dv", SCm, 0)
                d8 = DVm.inc(dv.tensor_tensor(sP, sP, P6[0:64, 0:130], ALU.add))
                if not prime:
                    k.waitabs("dv", DVm, d8)
                    DVm.inc(dv.tensor_copy(S16c[:], sP))

        def phaseB(prime, lv_base, lq_base):
            def body():
                k.wait("sc", PEm, 0)
                k.wait("sc", DVm, 0)
                k.wait("sc", SCm, 0)
                ssnap = sc.snap(r_g8s, min_val=0, max_val=H * 65 - 520)
                s32slice = S32[0:64, ds(ssnap, 520)]
                SCm.inc(sc.copy(S_cur[:], s32slice))
                vproj(lv_base + 16)
                for hp in range(4):
                    qkproj(lq_base[hp % 2] + 16 * (hp // 2 + 1), hp % 2, prime)
                    mixer_pair(hp, prime)
                k.wait("sc", DVm, 0)
                SCm.inc(sc.copy(s32slice, S_cur[:]))
                if not prime:
                    ysnap = sc.snap(r_g8y, min_val=0, max_val=12288)
                    SCm.inc(sc.copy(yat[:, ds(ysnap, 4096)], y_cur[:]))
                sc.reg_add(r_g8s, r_g8s, 520)
                sc.reg_add(r_g8y, r_g8y, 4096)

            k.forloop(4, ["pe", "dv", "sc"], body, extra={LV: 16, LQ0: 32, LQ1: 32})
            if not _DRY[0]:
                SC_.reg_add(r_g8s, r_g8s, -2080)
                SC_.reg_add(r_g8y, r_g8y, -16384)

        def phaseC(lv_base):
            def body():
                for tc in range(TCN):
                    par = tc % 2
                    bank = P0 if par == 0 else P1
                    k.waitabs("pe", LV, lv_base + 16 * 5)
                    k.wait("pe", SCm, 0)
                    k.wait("pe", SGC, -1)
                    k.wait("pe", DVm, 0)
                    for c in range(CI):
                        inst = pe.matmul(bank[:, 0:512],
                                         yat[:, c * SEG + tc * 128:c * SEG + (tc + 1) * 128],
                                         wvb[:, c * 512:(c + 1) * 512],
                                         start=(c == 0), stop=(c == CI - 1))
                    if tc == TCN - 1:
                        fv = FWV.inc(inst)
                        k.waitabs("sc", FWV, fv)
                    else:
                        p1 = PEm.inc(inst)
                        k.waitabs("sc", PEm, p1)
                    k.wait("sc", STCs[par], 0)
                    sg = SGC.inc(sc.copy(stg_c[:, par * 512:(par + 1) * 512], bank[:, 0:512]))
                    k.waitabs("sp", SGC, sg)
                    STCs[par].inc(sp.dma_start(
                        out=bass.AP(out.tensor, r_cst, [[DM, 128], [1, 512]]),
                        in_=stg_c[:, par * 512:(par + 1) * 512]), 16)
                    sp.reg_add(r_cst, r_cst, 128 * DM)
                sp.reg_add(r_cst, r_cst, 512 - TCN * 128 * DM)

            k.forloop(OBN, ["pe", "sc", "sp"], body, extra={LV: 16})

        def router():
            for tc in range(TCN):
                k.wait("pe", DVm, 0)
                k.wait("pe", SCm, 0)
                for c in range(CI):
                    inst = pe.matmul(P2[:, 0:8],
                                     xn_fm[:, c * SEG + tc * 128:c * SEG + (tc + 1) * 128],
                                     gw_sb[:, c * 8:(c + 1) * 8],
                                     start=(c == 0), stop=(c == CI - 1))
                p1 = PEm.inc(inst)
                k.waitabs("dv", PEm, p1)
                d1 = DVm.inc(dv.tensor_reduce(colA[:], P2[:, 0:8], AX.X, ALU.max, negate=True))
                k.waitabs("sc", DVm, d1)
                s1 = SCm.inc(sc.activation(r_s[:], P2[:, 0:8], AF.Exp, bias=colA[:]))
                k.waitabs("dv", SCm, s1)
                DVm.inc(dv.tensor_reduce(colB[:], r_s[:], AX.X, ALU.add))
                k.wait("dv", DVm, 0)
                DVm.inc(dv.reciprocal(colB[:], colB[:]))
                k.wait("dv", DVm, 0)
                DVm.inc(dv.tensor_scalar_mul(r_s[:], r_s[:], colB[:]))
                k.wait("dv", DVm, 0)
                DVm.inc(dv.tensor_reduce(colA[:], r_s[:], AX.X, ALU.max))
                k.wait("dv", DVm, 0)
                DVm.inc(dv.tensor_scalar(r_s2[:], r_s[:], colA[:], None, op0=ALU.is_ge))
                k.wait("dv", DVm, 0)
                DVm.inc(dv.scalar_tensor_tensor(r_s2[:], r_s2[:], -1e30, r_s[:],
                                                op0=ALU.mult, op1=ALU.add))
                k.wait("dv", DVm, 0)
                DVm.inc(dv.tensor_reduce(colB[:], r_s2[:], AX.X, ALU.max))
                k.wait("dv", DVm, 0)
                DVm.inc(dv.tensor_scalar(r_s2[:], r_s[:], colB[:], None, op0=ALU.is_ge))
                k.wait("dv", DVm, 0)
                DVm.inc(dv.tensor_tensor(r_s2[:], r_s2[:], r_s[:], ALU.mult))
                k.wait("dv", DVm, 0)
                DVm.inc(dv.tensor_reduce(colC[:], r_s2[:], AX.X, ALU.add))
                k.wait("dv", DVm, 0)
                DVm.inc(dv.tensor_scalar_add(colC[:], colC[:], 1e-20))
                k.wait("dv", DVm, 0)
                DVm.inc(dv.reciprocal(colC[:], colC[:]))
                k.wait("dv", DVm, 0)
                DVm.inc(dv.tensor_scalar_mul(cwv[:, 0:8, tc:tc + 1], r_s2[:], colC[:]))

        def phaseD(lq_base, ld_base, spd_base):
            def body():
                for eh in (0, 1):
                    k.wait("dv", DVm, 0)
                    cwsnap = dv.snap(r_cw, min_val=0, max_val=NE * TCN - 8)
                    DVm.inc(dv.tensor_copy(cwcur[:, eh * 8:(eh + 1) * 8], cw_sb[:, ds(cwsnap, 8)]))
                    dv.reg_add(r_cw, r_cw, 8)
                    for fc in range(FCN):
                        slot = fc % 2
                        gbank = P0 if slot == 0 else P1
                        ubank = P2 if slot == 0 else P3
                        for tb in range(TBN):
                            k.waitabs("pe", LQs[fc % 2],
                                      lq_base[fc % 2] + 16 * (8 + eh * 6 + fc // 2 + 1))
                            k.wait("pe", SCm, 0)
                            k.wait("pe", DVm, 0)
                            if fc == 0 and tb == 0:
                                k.wait("pe", SGC, 0)
                            for c in range(CI):
                                ig = pe.matmul(gbank[:, 0:512],
                                               wqg[:, slot * 4352 + c * 128:slot * 4352 + (c + 1) * 128],
                                               xn_fm[:, c * SEG + tb * 512:c * SEG + (tb + 1) * 512],
                                               start=(c == 0), stop=(c == CI - 1))
                            pg = PEm.inc(ig)
                            for c in range(CI):
                                iu = pe.matmul(ubank[:, 0:512],
                                               wqg[:, slot * 4352 + 2048 + c * 128:
                                                   slot * 4352 + 2048 + (c + 1) * 128],
                                               xn_fm[:, c * SEG + tb * 512:c * SEG + (tb + 1) * 512],
                                               start=(c == 0), stop=(c == CI - 1))
                            if tb == TBN - 1:
                                pu = ("FWQ", FWQ.inc(iu))
                            else:
                                pu = ("PEn", PEn.inc(iu))
                            k.waitabs("sc", PEm, pg)
                            k.wait("sc", DVm, 0)
                            s1 = SCm.inc(sc.activation(sg_sb[:, 0:512],
                                                       gbank[:, 0:512], AF.Silu))
                            k.waitabs("dv", SCm, s1)
                            k.waitabs("dv", FWQ if pu[0] == "FWQ" else PEn, pu[1])
                            if fc == 0 and tb == 0:
                                k.wait("dv", FWD, 0)
                            DVm.inc(dv.tensor_tensor(
                                h_fm[:, fc * SEG + tb * 512:fc * SEG + (tb + 1) * 512],
                                sg_sb[:, 0:512], ubank[:, 0:512], ALU.mult))
                    for obp in (0, 1):
                        for tc in range(TCN):
                            gidx = eh * 16 + obp * 8 + tc
                            slot4 = tc % 4
                            for oh in (0, 1):
                                ob = obp * 2 + oh
                                bank = P0 if oh == 0 else P1
                                k.waitabs("pe", LDs[ob % 2],
                                          ld_base[ob % 2] + 16 * (eh * 2 + ob // 2 + 1))
                                if tc == 0 or oh == 0:
                                    k.wait("pe", DVm, 0)
                                if tc == 0 or oh == 1:
                                    k.wait("pe", SGD, 0)
                                if obp == 0 and tc == 0 and oh == 0:
                                    k.waitabs("pe", LQ1,
                                              lq_base[1] + 16 * (8 + eh * 6 + 6))
                                first_dn = None
                                for fc in range(FCN):
                                    idn = pe.matmul(bank[:, 0:512],
                                                    h_fm[:, fc * SEG + tc * 128:fc * SEG + (tc + 1) * 128],
                                                    wdb[:, (ob % 2) * FCN * 512 + fc * 512:
                                                        (ob % 2) * FCN * 512 + (fc + 1) * 512],
                                                    start=(fc == 0), stop=(fc == FCN - 1))
                                    if fc == 0:
                                        first_dn = idn
                                if obp == 0 and tc == 0 and oh == 0:
                                    FWQ.inc(first_dn)
                                if tc == TCN - 1:
                                    fd = FWD.inc(idn)
                                    k.waitabs("dv", FWD, fd)
                                else:
                                    pd = PEm.inc(idn)
                                    k.waitabs("dv", PEm, pd)
                                k.waitabs("dv", SPD, spd_base + 16 * (gidx // 4))
                                inst = dv.tensor_scalar_mul(
                                    stg_d[:, slot4 * 1024 + oh * 512:slot4 * 1024 + (oh + 1) * 512],
                                    bank[:, 0:512],
                                    cwcur[:, eh * 8 + tc:eh * 8 + tc + 1])
                                if oh == 1:
                                    SGD.inc(inst)
                                else:
                                    DVm.inc(inst)

            k.forloop(NE // 2, ["pe", "dv", "sc"], body,
                      extra={LQ0: 192, LQ1: 192, LD0: 64, LD1: 64, SPD: 2 * 4 * 16})
            if not _DRY[0]:
                DV_.reg_add(r_cw, r_cw, -NE * TCN)

        def sp_dstores(sgd_base):
            SROW = 512 * DM

            def tcq_body():
                k.waitabs("sp", SGD, sgd_base + 4)
                apo = bass.AP(dscr.tensor, r_sd,
                              [[DM, 128], [128 * DM, 4], [512, 2], [1, 512]])
                SPD.inc(sp.dma_start(out=apo, in_=stg_d[:, 0:4096]), 16)
                k.wait("sp", SPD, 0)
                sp.reg_add(r_sd, r_sd, SROW)

            def obp_body():
                k.forloop(2, ["sp"], tcq_body, extra={SGD: 4})
                sp.reg_add(r_sd, r_sd, 1024 - 2 * SROW)

            def e_body():
                k.forloop(2, ["sp"], obp_body, extra={SGD: 8})
                sp.reg_add(r_sd, r_sd, 2 * SROW - 2048)

            k.forloop(NE, ["sp"], e_body, extra={SGD: 16})
            if not _DRY[0]:
                SP_.reg_add(r_sd, r_sd, -NE * 1024 * DM)

        def pool_acc(spd_base):
            k.wait("gp", STC0, 0)
            k.wait("gp", STC1, 0)
            if not _DRY[0]:
                GP_.reg_mov(r_uvo, 0)

            def e_body():
                k.waitabs("gp", SPD, spd_base + 64)
                api = bass.AP(dscr.tensor, r_uvo, [[2 * DM, 512], [1, DM]])
                apo = bass.AP(out.tensor, r_so, [[2 * DM, 512], [1, DM]])
                STD.inc(gp.dma_start(out=apo, in_=api, accum_op=ALU.add), 16)
                gp.reg_add(r_uvo, r_uvo, DM)
                gp.reg_add(r_so, r_so, DM)
                api2 = bass.AP(dscr.tensor, r_uvo, [[2 * DM, 512], [1, DM]])
                apo2 = bass.AP(out.tensor, r_so, [[2 * DM, 512], [1, DM]])
                STD.inc(gp.dma_start(out=apo2, in_=api2, accum_op=ALU.add), 16)
                gp.reg_add(r_so, r_so, -DM)
                k.wait("gp", STD, 0)
                gp.reg_add(r_uvo, r_uvo, 1024 * DM - DM)

            k.forloop(NE, ["gp"], e_body, extra={SPD: 64})
            if not _DRY[0]:
                GP_.reg_add(r_so, r_so, SEG * DM)

        gp.wait_ge(LC.h, LC.n)
        k.barrier()

        def prebody():
            lv0 = LV.n
            lq0 = [LQ0.n, LQ1.n]
            if not _DRY[0]:
                GP_.reg_mov(r_uvo, 0)
                GP_.reg_mov(r_uqg, 0)
            poolB_stream()
            phaseA(x_pre)
            phaseB(True, lv0, lq0)
            k.barrier()

        if "P" in parts:
            k.forloop(NPRE, ["pe", "dv", "sc", "gp", "sp"], prebody)

        SP_.reg_mov(r_xrow, 0)

        def segbody():
            lv0 = LV.n
            lq0 = [LQ0.n, LQ1.n]
            ld0 = [LD0.n, LD1.n]
            spd0 = SPD.n
            sgd0 = SGD.n
            if not _DRY[0]:
                GP_.reg_mov(r_uvo, 0)
                GP_.reg_mov(r_uqg, 0)
                GP_.reg_mov(r_uwd, 0)
                DV_.reg_mov(r_cw, 0)
            if "B" in parts:
                poolB_stream()
            if "C" in parts:
                pool_stream(4, wvo_th, CI * 512, r_uvo, [LV], FWV, [wvb[:]])

            def poolD():
                pool_stream(6, wpr_th, 4352, r_uqg, [LQ0, LQ1], FWQ,
                            [wqg[:, 0:4352], wqg[:, 4352:8704]])
                pool_stream(2, wd_th, FCN * 512, r_uwd, [LD0, LD1], FWD,
                            [wdb[:, 0:FCN * 512], wdb[:, FCN * 512:2 * FCN * 512]])

            if "D" in parts:
                k.forloop(NE, ["gp"], poolD, extra={FWQ: 12, FWD: 4})
            if "A" in parts:
                phaseA(x_own)
            if "B" in parts:
                phaseB(False, lv0, lq0)
            if "C" in parts:
                phaseC(lv0)
            if "R" in parts:
                router()
            if "D" in parts:
                phaseD(lq0, ld0, spd0)
            if "S" in parts:
                sp_dstores(sgd0)
                pool_acc(spd0)
            if "C" in parts and not _DRY[0]:
                SP_.reg_add(r_cst, r_cst, SEG * DM - 4 * 512)
            k.barrier()

        k.forloop(NSEG, ["pe", "dv", "sc", "gp", "sp"], segbody)

        if "C" in parts:
            gp.wait_ge(STC0.h, STC0.n)
            gp.wait_ge(STC1.h, STC1.n)
        for off in (0, DM):
            apo = bass.AP(out.tensor, off, [[2 * DM, TOK // 2], [1, DM]])
            api = bass.AP(x_own.tensor, off, [[2 * DM, TOK // 2], [1, DM]])
            STD.inc(gp.dma_start(out=apo, in_=api, accum_op=ALU.add), 16)
        gp.wait_ge(STD.h, STD.n)
    return nc



def _stage_weights(inp):
    f32 = np.float32
    asn = lambda a: np.ascontiguousarray(a, dtype=f32)

    Wq, Wk = np.asarray(inp["L0_attn_Wq"]), np.asarray(inp["L0_attn_Wk"])
    Wv, Wo = np.asarray(inp["L0_attn_Wv"]), np.asarray(inp["L0_attn_Wo"])
    Wfq, Wfk = np.asarray(inp["L0_attn_Wfq"]), np.asarray(inp["L0_attn_Wfk"])
    gwt = np.asarray(inp["L0_ffn_gate_weight"])
    wg, wu, wd = (np.asarray(inp["L0_ffn_w_gate"]), np.asarray(inp["L0_ffn_w_up"]),
                  np.asarray(inp["L0_ffn_w_down"]))
    sg, su, sd = (np.asarray(inp["L0_ffn_sw_gate"]), np.asarray(inp["L0_ffn_sw_up"]),
                  np.asarray(inp["L0_ffn_sw_down"]))

    def hp_tiles(W):
        return W.T.reshape(CI, 128, 16, 128).transpose(2, 1, 0, 3)

    wqk = np.stack([hp_tiles(Wq), hp_tiles(Wk)], axis=2).reshape(16, 128, 2 * CI * 128)
    wftail = np.zeros((16, 128, 256), dtype=f32)
    for hp in range(16):
        blk = np.zeros((64, 256), dtype=f32)
        for hloc in range(2):
            blk[:, hloc * 64:(hloc + 1) * 64] = Wfq[hp * 2 + hloc].astype(f32)
            blk[:, 128 + hloc * 64:128 + (hloc + 1) * 64] = Wfk[hp * 2 + hloc].astype(f32)
        wftail[hp, 0:64] = blk
        wftail[hp, 64:128] = blk
    wqk = np.concatenate([wqk, wftail], axis=2)

    def gu_tiles(g, u):
        gT = g.T.reshape(CI, 128, FCN, 128).transpose(2, 1, 0, 3)
        uT = u.T.reshape(CI, 128, FCN, 128).transpose(2, 1, 0, 3)
        return np.stack([gT, uT], axis=2).reshape(FCN, 128, 2 * CI * 128)

    wgu_list = [gu_tiles(wg[e], wu[e]) for e in range(8)]
    wgu_list.append(gu_tiles(sg[0:1408], su[0:1408]))
    wgu_list.append(gu_tiles(sg[1408:2816], su[1408:2816]))
    wgu_list = [np.concatenate([g, g[0:1]], axis=0) for g in wgu_list]
    wgu_all = np.concatenate(wgu_list, axis=0)
    wgu_all = np.concatenate([wgu_all, np.zeros((120, 128, 256), dtype=f32)], axis=2)
    wpr_t = asn(np.concatenate([wqk, wgu_all], axis=0))

    def big_tiles(W):
        return W.T.reshape(CI, 128, 4, 512).transpose(2, 1, 0, 3).reshape(4, 128, CI * 512)

    wvo_t = asn(np.concatenate([big_tiles(Wv), big_tiles(Wo)], axis=0))

    def d_tiles(W):
        return W.T.reshape(FCN, 128, OBN, 512).transpose(2, 1, 0, 3).reshape(OBN, 128, FCN * 512)

    wd_list = [d_tiles(wd[e]) for e in range(8)]
    wd_list.append(d_tiles(sd[:, 0:1408]))
    wd_list.append(d_tiles(sd[:, 1408:2816]))
    wd_t = asn(np.concatenate(wd_list, axis=0))

    gw_t = asn(gwt.T.reshape(CI, 128, 8).transpose(1, 0, 2).reshape(128, CI * 8))
    ident = np.eye(128, dtype=ml_dtypes.bfloat16)
    maskut = np.tile(np.triu(np.ones((128, 128))), (1, 2)).astype(ml_dtypes.bfloat16)
    cva = np.tile(np.asarray([CONST_VALS], dtype=np.float32), (128, 1))

    return dict(wvo_t=wvo_t, wpr_t=wpr_t, wd_t=wd_t, gw_t=gw_t,
                ident=ident, maskut=maskut, cvals=cva)


_NC_CACHE = [None]


def kernel(**inputs):
    x = np.ascontiguousarray(np.asarray(inputs["L0_attn_hidden_states"], dtype=np.float32))
    w = _stage_weights(inputs)
    if _NC_CACHE[0] is None:
        _NC_CACHE[0] = build_nc()
    nc = _NC_CACHE[0]
    in_maps = []
    for core in range(8):
        b, half = core // 2, core % 2
        m = dict(w)
        m["x_own"] = np.ascontiguousarray(x[b, half * TOK:(half + 1) * TOK, :])
        m["x_pre"] = np.ascontiguousarray(x[b, 0:TOK, :])
        m["flagd"] = np.full((128, 1), float(half), dtype=np.float32)
        in_maps.append(m)
    res = run_bass_kernel_spmd(nc, in_maps, core_ids=list(range(8)))
    outp = np.empty((B, Lq, DM), dtype=np.float32)
    for core in range(8):
        b, half = core // 2, core % 2
        outp[b, half * TOK:(half + 1) * TOK, :] = np.asarray(res.results[core]["out"])
    return outp


# version 05
# speedup vs baseline: 1.2126x; 1.0128x over previous
import numpy as np
import ml_dtypes
import contextlib
import concourse.bass as bass
import concourse.mybir as mybir
from concourse.bass import ds
from concourse.bass_utils import run_bass_kernel_spmd
from concourse.alu_op_type import AluOpType as ALU

F32 = mybir.dt.float32
BF16 = mybir.dt.bfloat16
AF = mybir.ActivationFunctionType
AX = mybir.AxisListType
ET = mybir.EngineType

B, Lq, DM = 4, 4096, 2048
TOK, SEG = 2048, 1024
NSEG, NPRE = 2, 2
TCN, TBN, CI = 8, 2, 16
H, HD, FD = 32, 64, 64
NE, FCN, OBN = 10, 11, 4
EPS_LN, EPS_AT = 1e-5, 1e-12
CONST_VALS = [EPS_LN, -0.5, 1.0 / DM, -1.0, EPS_AT, -1e30, 1e-20, 2.0]

ENAMES = {"pe": ET.PE, "dv": ET.DVE, "sc": ET.Activation, "gp": ET.Pool, "sp": ET.SP}
_DRY = [False]


class _MockInst:
    def then_inc(self, *a, **k):
        return self


_MOCK = _MockInst()


class Eng:
    def __init__(self, real):
        self._real = real

    def __getattr__(self, name):
        if _DRY[0]:
            if name == "snap":
                return lambda *a, **k: 0
            return lambda *a, **k: _MOCK
        return getattr(self._real, name)


class S:
    def __init__(self, nc, es, name):
        self.h = es.enter_context(nc.semaphore(name))
        self.n = 0
        self.name = name

    def inc(self, inst, k=1):
        self.n += k
        if inst is not None and not isinstance(inst, _MockInst):
            inst.then_inc(self.h, k)
        return self.n


class KB:
    def __init__(self, nc, es):
        self.nc = nc
        self.es = es
        self.sems = []
        attr = {"pe": "tensor", "dv": "vector", "sc": "scalar", "gp": "gpsimd", "sp": "sync"}
        self.eng = {n: Eng(getattr(nc, a)) for n, a in attr.items()}
        self.real = {n: getattr(nc, a) for n, a in attr.items()}
        self.rW = {}
        self.tmp = {}

    def sem(self, name):
        s = S(self.nc, self.es, name)
        self.sems.append(s)
        return s

    def declare_waiters(self, pairs):
        for en, sem in pairs:
            r = self.real[en].alloc_register(f"rW_{en}_{sem.name}")
            self.real[en].reg_mov(r, 0)
            self.rW[(en, sem)] = r
        for en in ENAMES:
            t = self.real[en].alloc_register(f"tmp_{en}")
            self.tmp[en] = t

    def wait(self, en, sem, off=0):
        self.waitabs(en, sem, sem.n + off)

    def waitabs(self, en, sem, static):
        if _DRY[0]:
            return
        e = self.real[en]
        key = (en, sem)
        if key in self.rW:
            e.reg_alu(self.tmp[en], self.rW[key], static, mybir.AluOpType.add)
            e.wait_ge(sem.h, self.tmp[en])
        else:
            assert static >= 0
            e.wait_ge(sem.h, static)

    def forloop(self, n, engines, body, extra=None):
        extra = extra or {}
        snap = [s.n for s in self.sems]
        if _DRY[0]:
            body()
            for s, s0 in zip(self.sems, snap):
                s.n = s0 + n * (s.n - s0)
            return
        _DRY[0] = True
        body()
        real_d = {s: s.n - s0 for s, s0 in zip(self.sems, snap)}
        for s, s0 in zip(self.sems, snap):
            s.n = s0
        _DRY[0] = False
        deltas = dict(real_d)
        for s, d in extra.items():
            deltas[s] = deltas.get(s, 0) + d

        nc = self.nc
        engs = [ENAMES[e] for e in engines]
        loop_id = nc.next_id()
        ls, le = f"kf_{loop_id}_loop", f"kf_{loop_id}_end"
        registers = nc.alloc_registers(f"kf_{loop_id}_i", engines=engs)
        nc.regs_mov(registers, 0)
        nc.br(ls, engines=engs)
        with nc.body(ls, valid_engines=engs):
            body()
            for en in engines:
                for (wen, sem), r in self.rW.items():
                    if wen == en and deltas.get(sem, 0):
                        self.real[en].reg_alu(r, r, deltas[sem], mybir.AluOpType.add)
            nc.regs_alu(registers, registers, 1, op=mybir.AluOpType.add)
            nc.br_lt(registers, n, on_true=ls, on_false=le, engines=engs)
        nc.switch_bb(le)
        for e, h in zip(registers.engines, registers.handles):
            nc.engines[e].free_register(h)
        for en in engines:
            for (wen, sem), r in self.rW.items():
                if wen == en and deltas.get(sem, 0):
                    self.real[en].reg_alu(r, r, -n * deltas[sem], mybir.AluOpType.add)
        for s, s0 in zip(self.sems, snap):
            s.n = s0 + n * real_d.get(s, 0)

    def barrier(self):
        if not _DRY[0]:
            self.nc.all_engine_barrier()


def build_nc(parts="PABCRDS"):
    nc = bass.Bass("TRN2", target_bir_lowering=False, debug=False)
    dt = nc.dram_tensor
    x_own = dt("x_own", [TOK, DM], F32, kind="ExternalInput").ap()
    x_pre = dt("x_pre", [TOK, DM], F32, kind="ExternalInput").ap()
    flagd = dt("flagd", [128, 1], F32, kind="ExternalInput").ap()
    wvo_t = dt("wvo_t", [8, 128, CI * 512], F32, kind="ExternalInput").ap()
    wpr_t = dt("wpr_t", [16 + NE * 12, 128, 4352], F32, kind="ExternalInput").ap()
    wd_t = dt("wd_t", [NE * OBN, 128, FCN * 512], F32, kind="ExternalInput").ap()
    gw_t = dt("gw_t", [128, CI * 8], F32, kind="ExternalInput").ap()
    ident = dt("ident", [128, 128], BF16, kind="ExternalInput").ap()
    maskut = dt("maskut", [128, 256], BF16, kind="ExternalInput").ap()
    cvals = dt("cvals", [128, 8], F32, kind="ExternalInput").ap()
    out = dt("out", [TOK, DM], F32, kind="ExternalOutput").ap()
    dscr = dt("dscr", [NE * 1024, DM], F32, kind="Internal").ap()

    es = contextlib.ExitStack()
    sbt = lambda name, shape, dty: es.enter_context(nc.sbuf_tensor(name, shape, dty))
    pst = lambda name, shape, dty: es.enter_context(nc.psum_tensor(name, shape, dty))

    xn_fm = sbt("xn_fm", [128, CI * SEG], BF16)
    yat = sbt("yat", [128, 16 * SEG], BF16)
    h_fm = sbt("h_fm", [128, FCN * SEG], BF16)
    v_ext = sbt("v_ext", [128, 8 * TCN * 65], BF16)
    qk_fm = sbt("qk_fm", [128, 2 * SEG], BF16)
    x_sb = sbt("x_sb", [128, DM], F32)
    wvb = sbt("wvb", [128, CI * 512], BF16)
    wqg = sbt("wqg", [128, 2 * 4352], BF16)
    wdb = sbt("wdb", [128, 2 * FCN * 512], BF16)
    stg_c = sbt("stg_c", [128, 2 * 512], F32)
    stg_d = sbt("stg_d", [128, 4 * 1024], F32)
    gw_sb = sbt("gw_sb", [128, CI * 8], BF16)
    cw_sb = sbt("cw_sb", [128, NE * TCN], F32)
    cwcur = sbt("cwcur", [128, 2 * TCN], F32)
    S32 = sbt("S32", [64, H * 65], F32)
    S_cur = sbt("S_cur", [64, 8 * 65], F32)
    S16c = sbt("S16c", [64, 130], BF16)
    y_cur = sbt("y_cur", [128, 4 * SEG], BF16)
    sg_sb = sbt("sg_sb", [128, 512], F32)
    id_sb = sbt("id_sb", [128, 128], BF16)
    mk2 = sbt("mk2", [128, 256], BF16)
    ones64 = sbt("ones64", [1, 64], F32)
    flag_sb = sbt("flag_sb", [128, 1], F32)
    cv_sb = sbt("cv_sb", [128, 8], F32)
    ex_k = sbt("ex_k", [128, 128], BF16)
    fq_tm = sbt("fq_tm", [128, 128], BF16)
    fk_tm = sbt("fk_tm", [128, 128], BF16)
    fqkT = sbt("fqkT", [64, 512], BF16)
    atm = sbt("atm", [128, 256], BF16)
    rd_f = sbt("rd_f", [1, 256], F32)
    rden = sbt("rden", [64, 256], F32)
    colA = sbt("colA", [128, 1], F32)
    colB = sbt("colB", [128, 1], F32)
    colC = sbt("colC", [128, 1], F32)
    colD = sbt("colD", [128, 1], F32)
    r_s = sbt("r_s", [128, 8], F32)
    r_s2 = sbt("r_s2", [128, 8], F32)

    for i, v in enumerate(CONST_VALS):
        nc.const_aps.register(F32, v, cv_sb[:, i:i + 1])

    P0 = pst("P0", [128, 512], F32)
    P1 = pst("P1", [128, 512], F32)
    P2 = pst("P2", [128, 512], F32)
    P3 = pst("P3", [128, 512], F32)
    P4 = pst("P4", [128, 512], F32)
    P5 = pst("P5", [128, 512], F32)
    P6 = pst("P6", [64, 512], F32)
    P7 = pst("P7", [128, 1024], BF16)

    k = KB(nc, es)
    LV = k.sem("LV")
    LQ0 = k.sem("LQ0")
    LQ1 = k.sem("LQ1")
    LQs = [LQ0, LQ1]
    LD0 = k.sem("LD0")
    LD1 = k.sem("LD1")
    LDs = [LD0, LD1]
    LXA = k.sem("LXA")
    FXA = k.sem("FXA")
    FWV = k.sem("FWV")
    FWQ = k.sem("FWQ")
    FWD = k.sem("FWD")
    PEm = k.sem("PEm")
    PEn = k.sem("PEn")
    DVm = k.sem("DVm")
    SCm = k.sem("SCm")
    STC0 = k.sem("STC0")
    STC1 = k.sem("STC1")
    STCs = [STC0, STC1]
    STD = k.sem("STD")
    SPD = k.sem("SPD")
    SGC = k.sem("SGC")
    SGD = k.sem("SGD")
    LC = k.sem("LC")

    k.declare_waiters([
        ("pe", LV), ("pe", LQ0), ("pe", LQ1), ("pe", LD0), ("pe", LD1),
        ("pe", DVm), ("pe", SCm), ("pe", FXA), ("pe", SGC), ("pe", SGD),
        ("dv", PEm), ("dv", PEn), ("dv", SCm), ("dv", LXA), ("dv", SPD), ("dv", FWD), ("dv", DVm),
        ("dv", FWV), ("dv", FWQ),
        ("sc", DVm), ("sc", PEm), ("sc", LXA), ("sc", STC0), ("sc", STC1), ("sc", SCm), ("sc", FWV),
        ("gp", FWV), ("gp", FWQ), ("gp", FWD), ("gp", SPD), ("gp", STD), ("gp", STC0), ("gp", STC1),
        ("sp", FXA), ("sp", SGC), ("sp", SGD), ("sp", SPD),
    ])
    pe, dv, sc, gp, sp = (k.eng[n] for n in ("pe", "dv", "sc", "gp", "sp"))
    PE_, DV_, SC_, GP_, SP_ = (k.real[n] for n in ("pe", "dv", "sc", "gp", "sp"))

    r_uvo = GP_.alloc_register("r_uvo")
    r_uqg = GP_.alloc_register("r_uqg")
    r_uwd = GP_.alloc_register("r_uwd")
    r_so = GP_.alloc_register("r_so")
    r_xrow = SP_.alloc_register("r_xrow")
    r_cst = SP_.alloc_register("r_cst")
    r_sd = SP_.alloc_register("r_sd")
    r_tcA = DV_.alloc_register("r_tcA")
    r_cw = DV_.alloc_register("r_cw")
    r_g8s = SC_.alloc_register("r_g8s")
    r_g8y = SC_.alloc_register("r_g8y")

    wvo_th = wvo_t.tensor
    wpr_th = wpr_t.tensor
    wd_th = wd_t.tensor
    xn4 = xn_fm[:].rearrange("p (c t) -> p c t", c=CI)
    p7v = P7[:].rearrange("p (j e) -> p j e", j=8)
    vve = v_ext[:].rearrange("p (a b c) -> p a b c", a=8, b=TCN, c=65)
    cwv = cw_sb[:].rearrange("p (a b) -> p a b", a=NE, b=TCN)

    with es:
        LC.inc(gp.dma_start(out=cv_sb[:], in_=cvals), 16)
        LC.inc(gp.dma_start(out=id_sb[:], in_=ident), 16)
        LC.inc(gp.dma_start(out=mk2[:], in_=maskut), 16)
        LC.inc(gp.dma_start(out=gw_sb[:], in_=gw_t), 16)
        LC.inc(gp.dma_start(out=flag_sb[:], in_=flagd), 16)
        sp.sem_inc(FWV.h, 1)
        FWV.n += 1
        sp.sem_inc(FWQ.h, 2)
        FWQ.n += 2
        sp.sem_inc(FWD.h, 2)
        FWD.n += 2

        sp.sem_inc(SGC.h, 2)
        SGC.n += 2
        dv.memset(S32[:], 0.0)
        dv.memset(vve[:, :, :, 64:65], 1.0)
        dv.memset(ones64[:], 1.0)
        dv.memset(cwv[:, 8:10, :], 1.0)
        for r in (r_uvo, r_uqg, r_uwd, r_so):
            gp.reg_mov(r, 0)
        for r in (r_xrow, r_cst, r_sd):
            sp.reg_mov(r, 0)
        for r in (r_tcA, r_cw):
            dv.reg_mov(r, 0)
        for r in (r_g8s, r_g8y):
            sc.reg_mov(r, 0)
        k.barrier()

        def pool_stream(n, src_th, elen, reg, sem_loads, sem_free, dests):

            def body():
                u0 = sum(s.n for s in sem_loads) // 16
                for j, (dst, sl) in enumerate(zip(dests, sem_loads)):
                    k.waitabs("gp", sem_free, u0 + j + 1)
                    ap = bass.AP(src_th, reg, [[elen, 128], [1, elen]])
                    sl.inc(gp.dma_start(out=dst, in_=ap), 16)
                    gp.reg_add(reg, reg, 128 * elen)

            k.forloop(n, ["gp"], body, extra={sem_free: len(dests)})

        def poolB_stream():

            def body():
                uv0 = LV.n // 16
                k.waitabs("gp", FWV, uv0 + 1)
                apv = bass.AP(wvo_th, r_uvo, [[CI * 512, 128], [1, CI * 512]])
                LV.inc(gp.dma_start(out=wvb[:], in_=apv), 16)
                gp.reg_add(r_uvo, r_uvo, 128 * CI * 512)
                uq0 = (LQ0.n + LQ1.n) // 16

                def qpair():
                    for j in (0, 1):
                        k.waitabs("gp", FWQ, uq0 + j + 1)
                        apq = bass.AP(wpr_th, r_uqg, [[4352, 128], [1, 4352]])
                        LQs[j].inc(gp.dma_start(out=wqg[:, j * 4352:(j + 1) * 4352], in_=apq), 16)
                        gp.reg_add(r_uqg, r_uqg, 128 * 4352)

                k.forloop(2, ["gp"], qpair, extra={FWQ: 2})

            k.forloop(4, ["gp"], body, extra={FWV: 1, FWQ: 4})


        def phaseA(xsrc):
            def body():
                k.wait("sp", FXA, 0)
                rr = sp.snap(r_xrow, min_val=0, max_val=TOK - 128)
                lx = LXA.inc(sp.dma_start(out=x_sb[:], in_=xsrc[ds(rr, 128), :]), 16)
                sp.reg_add(r_xrow, r_xrow, 128)
                k.waitabs("sc", LXA, lx)
                sq = SCm.inc(sc.activation(h_fm[:, 0:2048], x_sb[:], AF.Square, accum_out=colA[:]))
                k.waitabs("dv", LXA, lx)
                DVm.inc(dv.tensor_reduce(colB[:], x_sb[:], AX.X, ALU.add))
                k.wait("dv", DVm, 0)
                DVm.inc(dv.tensor_scalar_mul(colB[:], colB[:], 1.0 / DM))
                k.wait("dv", DVm, 0)
                k.waitabs("dv", SCm, sq)
                dv.tensor_scalar_mul(colC[:], colA[:], 1.0 / DM)
                DVm.inc(dv.tensor_tensor(colD[:], colB[:], colB[:], ALU.mult))
                k.wait("dv", DVm, 0)
                d3 = DVm.inc(dv.tensor_tensor(colC[:], colC[:], colD[:], ALU.subtract))
                k.waitabs("sc", DVm, d3)
                s2 = SCm.inc(sc.activation(colC[:], colC[:], AF.Ln, bias=EPS_LN))
                k.waitabs("sc", SCm, s2)
                s3 = SCm.inc(sc.activation(colC[:], colC[:], AF.Exp, scale=-0.5))
                k.waitabs("dv", SCm, s3)
                d4 = DVm.inc(dv.tensor_scalar(colD[:], colB[:], colC[:], -1.0,
                                              op0=ALU.mult, op1=ALU.mult))
                k.waitabs("sc", DVm, d4)
                k.wait("sc", PEm, 0)
                fx = FXA.inc(sc.activation(h_fm[:, 2048:4096], x_sb[:], AF.Identity,
                                           bias=colD[:], scale=colC[:]))
                toff = dv.snap(r_tcA, min_val=0, max_val=SEG - 128)
                for half in (0, 1):
                    k.waitabs("pe", FXA, fx)
                    k.wait("pe", DVm, 0)
                    inst = None
                    for j in range(8):
                        c = half * 8 + j
                        inst = pe.transpose(P7[:, j * 128:(j + 1) * 128],
                                            h_fm[:, 2048 + c * 128:2048 + (c + 1) * 128],
                                            id_sb[:])
                    p1 = PEm.inc(inst)
                    k.waitabs("dv", PEm, p1)
                    DVm.inc(dv.tensor_copy(xn4[:, half * 8:(half + 1) * 8, ds(toff, 128)],
                                           p7v[:, :, :]))
                dv.reg_add(r_tcA, r_tcA, 128)

            k.forloop(TCN, ["pe", "dv", "sc", "sp"], body)
            if not _DRY[0]:
                DV_.reg_add(r_tcA, r_tcA, -SEG)

        def vproj(lv_abs):
            for tc in range(TCN):
                bank = P0 if tc % 2 == 0 else P1
                bk4 = bank[:].rearrange("p (a b) -> p a b", a=8)
                k.waitabs("pe", LV, lv_abs)
                k.wait("pe", DVm, 0)
                for c in range(CI):
                    inst = pe.matmul(bank[:, 0:512],
                                     xn_fm[:, c * SEG + tc * 128:c * SEG + (tc + 1) * 128],
                                     wvb[:, c * 512:(c + 1) * 512],
                                     start=(c == 0), stop=(c == CI - 1))
                if tc == TCN - 1:
                    fv = FWV.inc(inst)
                    k.waitabs("dv", FWV, fv)
                else:
                    p1 = PEm.inc(inst)
                    k.waitabs("dv", PEm, p1)
                DVm.inc(dv.tensor_copy(vve[:, :, tc, 0:64], bk4[:, :, :]))

        def qkproj(lq_abs, slot, prime):
            ms = (1,) if prime else (0, 1)
            for m in ms:
                for tb in range(TBN):
                    bank = P0 if tb == 0 else P1
                    k.waitabs("pe", LQs[slot], lq_abs)
                    k.wait("pe", DVm, 0)
                    for c in range(CI):
                        inst = pe.matmul(bank[:, 0:512],
                                         wqg[:, slot * 4352 + m * 2048 + c * 128:
                                             slot * 4352 + m * 2048 + (c + 1) * 128],
                                         xn_fm[:, c * SEG + tb * 512:c * SEG + (tb + 1) * 512],
                                         start=(c == 0), stop=(c == CI - 1))
                    p1 = PEm.inc(inst)
                    k.waitabs("dv", PEm, p1)
                    DVm.inc(dv.tensor_copy(qk_fm[:, m * SEG + tb * 512:m * SEG + (tb + 1) * 512],
                                           bank[:, 0:512]))

        def mixer_pair(hp, prime):
            hA, hB = 2 * (hp % 4), 2 * (hp % 4) + 1
            slot = hp % 2
            wfq_A = wqg[0:64, slot * 4352 + 4096:slot * 4352 + 4096 + 64]
            wfq_B = wqg[64:128, slot * 4352 + 4160:slot * 4352 + 4160 + 64]
            wfk_A = wqg[0:64, slot * 4352 + 4224:slot * 4352 + 4224 + 64]
            wfk_B = wqg[64:128, slot * 4352 + 4288:slot * 4352 + 4288 + 64]
            sP = S_cur[0:64, hA * 65:hA * 65 + 130]
            if not prime:
                k.wait("dv", SCm, 0)
                k.wait("dv", DVm, 0)
                DVm.inc(dv.tensor_copy(S16c[:], sP))
            for c in range(TCN):
                if c == 0:
                    k.wait("pe", DVm, 0)
                    k.wait("pe", SCm, 0)
                else:
                    k.wait("pe", DVm, -4 if prime else -10)
                    k.wait("pe", SCm, 0 if prime else -1)
                if not prime:
                    pe.matmul(P4[:, 0:64], qk_fm[0:64, c * 128:(c + 1) * 128],
                              wfq_A, start=True, stop=True)
                pe.matmul(P4[:, 64:128], qk_fm[0:64, SEG + c * 128:SEG + (c + 1) * 128],
                          wfk_A, start=True, stop=True)
                if not prime:
                    pe.matmul(P3[:, 256:320], qk_fm[64:128, c * 128:(c + 1) * 128],
                              wfq_B, start=True, stop=True)
                p1 = PEm.inc(pe.matmul(P3[:, 320:384],
                                       qk_fm[64:128, SEG + c * 128:SEG + (c + 1) * 128],
                                       wfk_B, start=True, stop=True))
                k.waitabs("dv", PEm, p1)
                if prime:
                    dv.tensor_reduce(r_s[:, 1:2], P4[:, 64:128], AX.X, ALU.max, negate=True)
                    d1 = DVm.inc(dv.tensor_reduce(r_s[:, 3:4], P3[:, 320:384],
                                                  AX.X, ALU.max, negate=True))
                else:
                    dv.tensor_reduce(
                        r_s[:, 0:2], P4[:, 0:128].rearrange("p (a b) -> p a b", b=64),
                        AX.X, ALU.max, negate=True)
                    d1 = DVm.inc(dv.tensor_reduce(
                        r_s[:, 2:4], P3[:, 256:384].rearrange("p (a b) -> p a b", b=64),
                        AX.X, ALU.max, negate=True))
                k.waitabs("sc", DVm, d1)
                if not prime:
                    sc.activation(fq_tm[:, 0:64], P4[:, 0:64], AF.Exp, bias=r_s[:, 0:1])
                    sc.activation(fq_tm[:, 64:128], P3[:, 256:320], AF.Exp, bias=r_s[:, 2:3])
                sc.activation(ex_k[:, 0:64], P4[:, 64:128], AF.Exp, bias=r_s[:, 1:2])
                s1 = SCm.inc(sc.activation(ex_k[:, 64:128], P3[:, 320:384],
                                           AF.Exp, bias=r_s[:, 3:4]))
                k.waitabs("dv", SCm, s1)
                DVm.inc(dv.tensor_reduce(
                    r_s2[:, 0:2], ex_k[:].rearrange("p (a b) -> p a b", b=64),
                    AX.X, ALU.add))
                k.wait("dv", DVm, 0)
                DVm.inc(dv.reciprocal(r_s2[:, 0:2], r_s2[:, 0:2]))
                k.wait("dv", DVm, 0)
                if prime:
                    dv.tensor_scalar(fk_tm[:, 0:64], ex_k[:, 0:64], r_s2[:, 0:1],
                                     flag_sb[:], op0=ALU.mult, op1=ALU.mult)
                    dfk = DVm.inc(dv.tensor_scalar(fk_tm[:, 64:128], ex_k[:, 64:128],
                                                   r_s2[:, 1:2], flag_sb[:],
                                                   op0=ALU.mult, op1=ALU.mult))
                else:
                    dv.tensor_scalar_mul(fk_tm[:, 0:64], ex_k[:, 0:64], r_s2[:, 0:1])
                    dfk = DVm.inc(dv.tensor_scalar_mul(fk_tm[:, 64:128], ex_k[:, 64:128],
                                                       r_s2[:, 1:2]))

                if not prime:
                    k.waitabs("pe", DVm, dfk)
                    pe.transpose(P7[0:64, 0:128], fq_tm[:, 0:64], id_sb[:])
                    pe.transpose(P7[0:64, 128:256], fq_tm[:, 64:128], id_sb[:])
                    pe.transpose(P7[0:64, 256:384], fk_tm[:, 0:64], id_sb[:])
                    p2 = PEm.inc(pe.transpose(P7[0:64, 384:512], fk_tm[:, 64:128], id_sb[:]))
                    k.waitabs("sc", PEm, p2)
                    s2 = SCm.inc(sc.copy(fqkT[0:64, 0:512], P7[0:64, 0:512]))
                    k.waitabs("pe", SCm, s2)
                    pe.matmul(P3[:, 0:128], fqkT[0:64, 256:384], fqkT[0:64, 0:128],
                              start=True, stop=True)
                    p3 = PEm.inc(pe.matmul(P3[:, 128:256], fqkT[0:64, 384:512],
                                           fqkT[0:64, 128:256], start=True, stop=True))
                    k.waitabs("dv", PEm, p3)
                    d4 = DVm.inc(dv.tensor_tensor(atm[:, 0:256], P3[:, 0:256],
                                                  mk2[:, 0:256], ALU.mult))
                    k.waitabs("pe", DVm, d4)
                    pe.matmul(P5[0:64, 256:384],
                              v_ext[:, (hA * TCN + c) * 65:(hA * TCN + c) * 65 + 64],
                              atm[:, 0:128], start=True, stop=False)
                    pe.matmul(P5[0:64, 256:384], S16c[0:64, 0:64], fqkT[0:64, 0:128],
                              start=False, stop=True)
                    pe.matmul(P5[0:64, 384:512],
                              v_ext[:, (hB * TCN + c) * 65:(hB * TCN + c) * 65 + 64],
                              atm[:, 128:256], start=True, stop=False)
                    pe.matmul(P5[0:64, 384:512], S16c[0:64, 65:129], fqkT[0:64, 128:256],
                              start=False, stop=True)
                    pe.matmul(P2[0:1, 0:128], mk2[:, 127:128], atm[:, 0:128],
                              start=True, stop=False)
                    pe.matmul(P2[0:1, 0:128], S16c[0:64, 64:65], fqkT[0:64, 0:128],
                              start=False, stop=True)
                    pe.matmul(P2[0:1, 128:256], mk2[:, 127:128], atm[:, 128:256],
                              start=True, stop=False)
                    p4 = PEm.inc(pe.matmul(P2[0:1, 128:256], S16c[0:64, 129:130],
                                           fqkT[0:64, 128:256], start=False, stop=True))
                    k.waitabs("dv", PEm, p4)
                    DVm.inc(dv.tensor_scalar_add(rd_f[:], P2[0:1, 0:256], EPS_AT))
                    k.wait("dv", DVm, 0)
                    d6 = DVm.inc(dv.reciprocal(rd_f[:], rd_f[:]))
                    k.waitabs("pe", DVm, d6)
                    pe.matmul(P2[0:64, 256:384], ones64[:], rd_f[0:1, 0:128],
                              start=True, stop=True)
                    p5 = PEm.inc(pe.matmul(P2[0:64, 384:512], ones64[:], rd_f[0:1, 128:256],
                                           start=True, stop=True))
                    k.waitabs("dv", PEm, p5)
                    DVm.inc(dv.tensor_copy(rden[:, 0:256], P2[0:64, 256:512]))
                    k.wait("dv", DVm, 0)
                    k.wait("dv", SCm, 0)
                    cblk = (hp % 4) * SEG + c * 128
                    dv.tensor_tensor(y_cur[0:64, cblk:cblk + 128],
                                     P5[0:64, 256:384], rden[:, 0:128], ALU.mult)
                    DVm.inc(dv.tensor_tensor(y_cur[64:128, cblk:cblk + 128],
                                             P5[0:64, 384:512], rden[:, 128:256], ALU.mult))
                k.waitabs("pe", DVm, dfk)
                pe.matmul(P6[0:64, 0:65], fk_tm[:, 0:64],
                          v_ext[:, (hA * TCN + c) * 65:(hA * TCN + c + 1) * 65],
                          start=True, stop=True)
                inst6 = pe.matmul(P6[0:64, 65:130], fk_tm[:, 64:128],
                                  v_ext[:, (hB * TCN + c) * 65:(hB * TCN + c + 1) * 65],
                                  start=True, stop=True)
                if c == TCN - 1:
                    f6 = FWQ.inc(inst6)
                    k.waitabs("dv", FWQ, f6)
                else:
                    p6 = PEm.inc(inst6)
                    k.waitabs("dv", PEm, p6)
                k.wait("dv", SCm, 0)
                d8 = DVm.inc(dv.tensor_tensor(sP, sP, P6[0:64, 0:130], ALU.add))
                if not prime:
                    k.waitabs("dv", DVm, d8)
                    DVm.inc(dv.tensor_copy(S16c[:], sP))

        def phaseB(prime, lv_base, lq_base):
            def body():
                k.wait("sc", PEm, 0)
                k.wait("sc", DVm, 0)
                k.wait("sc", SCm, 0)
                ssnap = sc.snap(r_g8s, min_val=0, max_val=H * 65 - 520)
                s32slice = S32[0:64, ds(ssnap, 520)]
                SCm.inc(sc.copy(S_cur[:], s32slice))
                vproj(lv_base + 16)
                for hp in range(4):
                    qkproj(lq_base[hp % 2] + 16 * (hp // 2 + 1), hp % 2, prime)
                    mixer_pair(hp, prime)
                k.wait("sc", DVm, 0)
                SCm.inc(sc.copy(s32slice, S_cur[:]))
                if not prime:
                    ysnap = sc.snap(r_g8y, min_val=0, max_val=12288)
                    SCm.inc(sc.copy(yat[:, ds(ysnap, 4096)], y_cur[:]))
                sc.reg_add(r_g8s, r_g8s, 520)
                sc.reg_add(r_g8y, r_g8y, 4096)

            k.forloop(4, ["pe", "dv", "sc"], body, extra={LV: 16, LQ0: 32, LQ1: 32})
            if not _DRY[0]:
                SC_.reg_add(r_g8s, r_g8s, -2080)
                SC_.reg_add(r_g8y, r_g8y, -16384)

        def phaseC(lv_base):
            def body():
                for tc in range(TCN):
                    par = tc % 2
                    bank = P0 if par == 0 else P1
                    k.waitabs("pe", LV, lv_base + 16 * 5)
                    k.wait("pe", SCm, 0)
                    k.wait("pe", SGC, -1)
                    k.wait("pe", DVm, 0)
                    for c in range(CI):
                        inst = pe.matmul(bank[:, 0:512],
                                         yat[:, c * SEG + tc * 128:c * SEG + (tc + 1) * 128],
                                         wvb[:, c * 512:(c + 1) * 512],
                                         start=(c == 0), stop=(c == CI - 1))
                    if tc == TCN - 1:
                        fv = FWV.inc(inst)
                        k.waitabs("sc", FWV, fv)
                    else:
                        p1 = PEm.inc(inst)
                        k.waitabs("sc", PEm, p1)
                    k.wait("sc", STCs[par], 0)
                    sg = SGC.inc(sc.copy(stg_c[:, par * 512:(par + 1) * 512], bank[:, 0:512]))
                    k.waitabs("sp", SGC, sg)
                    STCs[par].inc(sp.dma_start(
                        out=bass.AP(out.tensor, r_cst, [[DM, 128], [1, 512]]),
                        in_=stg_c[:, par * 512:(par + 1) * 512]), 16)
                    sp.reg_add(r_cst, r_cst, 128 * DM)
                sp.reg_add(r_cst, r_cst, 512 - TCN * 128 * DM)

            k.forloop(OBN, ["pe", "sc", "sp"], body, extra={LV: 16})

        def router():
            for tc in range(TCN):
                k.wait("pe", DVm, 0)
                k.wait("pe", SCm, 0)
                for c in range(CI):
                    inst = pe.matmul(P2[:, 0:8],
                                     xn_fm[:, c * SEG + tc * 128:c * SEG + (tc + 1) * 128],
                                     gw_sb[:, c * 8:(c + 1) * 8],
                                     start=(c == 0), stop=(c == CI - 1))
                p1 = PEm.inc(inst)
                k.waitabs("dv", PEm, p1)
                d1 = DVm.inc(dv.tensor_reduce(colA[:], P2[:, 0:8], AX.X, ALU.max, negate=True))
                k.waitabs("sc", DVm, d1)
                s1 = SCm.inc(sc.activation(r_s[:], P2[:, 0:8], AF.Exp, bias=colA[:]))
                k.waitabs("dv", SCm, s1)
                DVm.inc(dv.tensor_reduce(colB[:], r_s[:], AX.X, ALU.add))
                k.wait("dv", DVm, 0)
                DVm.inc(dv.reciprocal(colB[:], colB[:]))
                k.wait("dv", DVm, 0)
                DVm.inc(dv.tensor_scalar_mul(r_s[:], r_s[:], colB[:]))
                k.wait("dv", DVm, 0)
                DVm.inc(dv.tensor_reduce(colA[:], r_s[:], AX.X, ALU.max))
                k.wait("dv", DVm, 0)
                DVm.inc(dv.tensor_scalar(r_s2[:], r_s[:], colA[:], None, op0=ALU.is_ge))
                k.wait("dv", DVm, 0)
                DVm.inc(dv.scalar_tensor_tensor(r_s2[:], r_s2[:], -1e30, r_s[:],
                                                op0=ALU.mult, op1=ALU.add))
                k.wait("dv", DVm, 0)
                DVm.inc(dv.tensor_reduce(colB[:], r_s2[:], AX.X, ALU.max))
                k.wait("dv", DVm, 0)
                DVm.inc(dv.tensor_scalar(r_s2[:], r_s[:], colB[:], None, op0=ALU.is_ge))
                k.wait("dv", DVm, 0)
                DVm.inc(dv.tensor_tensor(r_s2[:], r_s2[:], r_s[:], ALU.mult))
                k.wait("dv", DVm, 0)
                DVm.inc(dv.tensor_reduce(colC[:], r_s2[:], AX.X, ALU.add))
                k.wait("dv", DVm, 0)
                DVm.inc(dv.tensor_scalar_add(colC[:], colC[:], 1e-20))
                k.wait("dv", DVm, 0)
                DVm.inc(dv.reciprocal(colC[:], colC[:]))
                k.wait("dv", DVm, 0)
                DVm.inc(dv.tensor_scalar_mul(cwv[:, 0:8, tc:tc + 1], r_s2[:], colC[:]))

        def phaseD(lq_base, ld_base, spd_base):
            def body():
                for eh in (0, 1):
                    k.wait("dv", DVm, 0)
                    cwsnap = dv.snap(r_cw, min_val=0, max_val=NE * TCN - 8)
                    DVm.inc(dv.tensor_copy(cwcur[:, eh * 8:(eh + 1) * 8], cw_sb[:, ds(cwsnap, 8)]))
                    dv.reg_add(r_cw, r_cw, 8)
                    for fc in range(FCN):
                        slot = fc % 2
                        gbank = P0 if slot == 0 else P1
                        ubank = P2 if slot == 0 else P3
                        for tb in range(TBN):
                            k.waitabs("pe", LQs[fc % 2],
                                      lq_base[fc % 2] + 16 * (8 + eh * 6 + fc // 2 + 1))
                            if fc >= 2 and tb == 0:
                                k.wait("pe", SCm, -2)
                                k.wait("pe", DVm, -2)
                            else:
                                k.wait("pe", SCm, 0)
                                k.wait("pe", DVm, 0)
                            if fc == 0 and tb == 0:
                                k.wait("pe", SGC, 0)
                            for c in range(CI):
                                ig = pe.matmul(gbank[:, 0:512],
                                               wqg[:, slot * 4352 + c * 128:slot * 4352 + (c + 1) * 128],
                                               xn_fm[:, c * SEG + tb * 512:c * SEG + (tb + 1) * 512],
                                               start=(c == 0), stop=(c == CI - 1))
                            pg = PEm.inc(ig)
                            for c in range(CI):
                                iu = pe.matmul(ubank[:, 0:512],
                                               wqg[:, slot * 4352 + 2048 + c * 128:
                                                   slot * 4352 + 2048 + (c + 1) * 128],
                                               xn_fm[:, c * SEG + tb * 512:c * SEG + (tb + 1) * 512],
                                               start=(c == 0), stop=(c == CI - 1))
                            if tb == TBN - 1:
                                pu = ("FWQ", FWQ.inc(iu))
                            else:
                                pu = ("PEn", PEn.inc(iu))
                            k.waitabs("sc", PEm, pg)
                            k.wait("sc", DVm, 0)
                            s1 = SCm.inc(sc.activation(sg_sb[:, 0:512],
                                                       gbank[:, 0:512], AF.Silu))
                            k.waitabs("dv", SCm, s1)
                            k.waitabs("dv", FWQ if pu[0] == "FWQ" else PEn, pu[1])
                            if fc == 0 and tb == 0:
                                k.wait("dv", FWD, 0)
                            DVm.inc(dv.tensor_tensor(
                                h_fm[:, fc * SEG + tb * 512:fc * SEG + (tb + 1) * 512],
                                sg_sb[:, 0:512], ubank[:, 0:512], ALU.mult))
                    for obp in (0, 1):
                        for tc in range(TCN):
                            gidx = eh * 16 + obp * 8 + tc
                            slot4 = tc % 4
                            for oh in (0, 1):
                                ob = obp * 2 + oh
                                bank = P0 if oh == 0 else P1
                                k.waitabs("pe", LDs[ob % 2],
                                          ld_base[ob % 2] + 16 * (eh * 2 + ob // 2 + 1))
                                if tc == 0 or oh == 0:
                                    k.wait("pe", DVm, 0)
                                if tc == 0 or oh == 1:
                                    k.wait("pe", SGD, 0)
                                if obp == 0 and tc == 0 and oh == 0:
                                    k.waitabs("pe", LQ1,
                                              lq_base[1] + 16 * (8 + eh * 6 + 6))
                                first_dn = None
                                for fc in range(FCN):
                                    idn = pe.matmul(bank[:, 0:512],
                                                    h_fm[:, fc * SEG + tc * 128:fc * SEG + (tc + 1) * 128],
                                                    wdb[:, (ob % 2) * FCN * 512 + fc * 512:
                                                        (ob % 2) * FCN * 512 + (fc + 1) * 512],
                                                    start=(fc == 0), stop=(fc == FCN - 1))
                                    if fc == 0:
                                        first_dn = idn
                                if obp == 0 and tc == 0 and oh == 0:
                                    FWQ.inc(first_dn)
                                if tc == TCN - 1:
                                    fd = FWD.inc(idn)
                                    k.waitabs("dv", FWD, fd)
                                else:
                                    pd = PEm.inc(idn)
                                    k.waitabs("dv", PEm, pd)
                                k.waitabs("dv", SPD, spd_base + 16 * (gidx // 4))
                                inst = dv.tensor_scalar_mul(
                                    stg_d[:, slot4 * 1024 + oh * 512:slot4 * 1024 + (oh + 1) * 512],
                                    bank[:, 0:512],
                                    cwcur[:, eh * 8 + tc:eh * 8 + tc + 1])
                                if oh == 1:
                                    SGD.inc(inst)
                                else:
                                    DVm.inc(inst)

            k.forloop(NE // 2, ["pe", "dv", "sc"], body,
                      extra={LQ0: 192, LQ1: 192, LD0: 64, LD1: 64, SPD: 2 * 4 * 16})
            if not _DRY[0]:
                DV_.reg_add(r_cw, r_cw, -NE * TCN)

        def sp_dstores(sgd_base):
            SROW = 512 * DM

            def tcq_body():
                k.waitabs("sp", SGD, sgd_base + 4)
                apo = bass.AP(dscr.tensor, r_sd,
                              [[DM, 128], [128 * DM, 4], [512, 2], [1, 512]])
                SPD.inc(sp.dma_start(out=apo, in_=stg_d[:, 0:4096]), 16)
                k.wait("sp", SPD, 0)
                sp.reg_add(r_sd, r_sd, SROW)

            def obp_body():
                k.forloop(2, ["sp"], tcq_body, extra={SGD: 4})
                sp.reg_add(r_sd, r_sd, 1024 - 2 * SROW)

            def e_body():
                k.forloop(2, ["sp"], obp_body, extra={SGD: 8})
                sp.reg_add(r_sd, r_sd, 2 * SROW - 2048)

            k.forloop(NE, ["sp"], e_body, extra={SGD: 16})
            if not _DRY[0]:
                SP_.reg_add(r_sd, r_sd, -NE * 1024 * DM)

        def pool_acc(spd_base):
            k.wait("gp", STC0, 0)
            k.wait("gp", STC1, 0)
            if not _DRY[0]:
                GP_.reg_mov(r_uvo, 0)

            def e_body():
                k.waitabs("gp", SPD, spd_base + 64)
                api = bass.AP(dscr.tensor, r_uvo, [[2 * DM, 512], [1, DM]])
                apo = bass.AP(out.tensor, r_so, [[2 * DM, 512], [1, DM]])
                STD.inc(gp.dma_start(out=apo, in_=api, accum_op=ALU.add), 16)
                gp.reg_add(r_uvo, r_uvo, DM)
                gp.reg_add(r_so, r_so, DM)
                api2 = bass.AP(dscr.tensor, r_uvo, [[2 * DM, 512], [1, DM]])
                apo2 = bass.AP(out.tensor, r_so, [[2 * DM, 512], [1, DM]])
                STD.inc(gp.dma_start(out=apo2, in_=api2, accum_op=ALU.add), 16)
                gp.reg_add(r_so, r_so, -DM)
                k.wait("gp", STD, 0)
                gp.reg_add(r_uvo, r_uvo, 1024 * DM - DM)

            k.forloop(NE, ["gp"], e_body, extra={SPD: 64})
            if not _DRY[0]:
                GP_.reg_add(r_so, r_so, SEG * DM)

        gp.wait_ge(LC.h, LC.n)
        k.barrier()

        def prebody():
            lv0 = LV.n
            lq0 = [LQ0.n, LQ1.n]
            if not _DRY[0]:
                GP_.reg_mov(r_uvo, 0)
                GP_.reg_mov(r_uqg, 0)
            poolB_stream()
            phaseA(x_pre)
            phaseB(True, lv0, lq0)
            k.barrier()

        if "P" in parts:
            k.forloop(NPRE, ["pe", "dv", "sc", "gp", "sp"], prebody)

        SP_.reg_mov(r_xrow, 0)

        def segbody():
            lv0 = LV.n
            lq0 = [LQ0.n, LQ1.n]
            ld0 = [LD0.n, LD1.n]
            spd0 = SPD.n
            sgd0 = SGD.n
            if not _DRY[0]:
                GP_.reg_mov(r_uvo, 0)
                GP_.reg_mov(r_uqg, 0)
                GP_.reg_mov(r_uwd, 0)
                DV_.reg_mov(r_cw, 0)
            if "B" in parts:
                poolB_stream()
            if "C" in parts:
                pool_stream(4, wvo_th, CI * 512, r_uvo, [LV], FWV, [wvb[:]])

            def poolD():
                pool_stream(6, wpr_th, 4352, r_uqg, [LQ0, LQ1], FWQ,
                            [wqg[:, 0:4352], wqg[:, 4352:8704]])
                pool_stream(2, wd_th, FCN * 512, r_uwd, [LD0, LD1], FWD,
                            [wdb[:, 0:FCN * 512], wdb[:, FCN * 512:2 * FCN * 512]])

            if "D" in parts:
                k.forloop(NE, ["gp"], poolD, extra={FWQ: 12, FWD: 4})
            if "A" in parts:
                phaseA(x_own)
            if "B" in parts:
                phaseB(False, lv0, lq0)
            if "C" in parts:
                phaseC(lv0)
            if "R" in parts:
                router()
            if "D" in parts:
                phaseD(lq0, ld0, spd0)
            if "S" in parts:
                sp_dstores(sgd0)
                pool_acc(spd0)
            if "C" in parts and not _DRY[0]:
                SP_.reg_add(r_cst, r_cst, SEG * DM - 4 * 512)
            k.barrier()

        k.forloop(NSEG, ["pe", "dv", "sc", "gp", "sp"], segbody)

        if "C" in parts:
            gp.wait_ge(STC0.h, STC0.n)
            gp.wait_ge(STC1.h, STC1.n)
        for off in (0, DM):
            apo = bass.AP(out.tensor, off, [[2 * DM, TOK // 2], [1, DM]])
            api = bass.AP(x_own.tensor, off, [[2 * DM, TOK // 2], [1, DM]])
            STD.inc(gp.dma_start(out=apo, in_=api, accum_op=ALU.add), 16)
        gp.wait_ge(STD.h, STD.n)
    return nc



def _stage_weights(inp):
    f32 = np.float32
    asn = lambda a: np.ascontiguousarray(a, dtype=f32)

    Wq, Wk = np.asarray(inp["L0_attn_Wq"]), np.asarray(inp["L0_attn_Wk"])
    Wv, Wo = np.asarray(inp["L0_attn_Wv"]), np.asarray(inp["L0_attn_Wo"])
    Wfq, Wfk = np.asarray(inp["L0_attn_Wfq"]), np.asarray(inp["L0_attn_Wfk"])
    gwt = np.asarray(inp["L0_ffn_gate_weight"])
    wg, wu, wd = (np.asarray(inp["L0_ffn_w_gate"]), np.asarray(inp["L0_ffn_w_up"]),
                  np.asarray(inp["L0_ffn_w_down"]))
    sg, su, sd = (np.asarray(inp["L0_ffn_sw_gate"]), np.asarray(inp["L0_ffn_sw_up"]),
                  np.asarray(inp["L0_ffn_sw_down"]))

    def hp_tiles(W):
        return W.T.reshape(CI, 128, 16, 128).transpose(2, 1, 0, 3)

    wqk = np.stack([hp_tiles(Wq), hp_tiles(Wk)], axis=2).reshape(16, 128, 2 * CI * 128)
    wftail = np.zeros((16, 128, 256), dtype=f32)
    for hp in range(16):
        blk = np.zeros((64, 256), dtype=f32)
        for hloc in range(2):
            blk[:, hloc * 64:(hloc + 1) * 64] = Wfq[hp * 2 + hloc].astype(f32)
            blk[:, 128 + hloc * 64:128 + (hloc + 1) * 64] = Wfk[hp * 2 + hloc].astype(f32)
        wftail[hp, 0:64] = blk
        wftail[hp, 64:128] = blk
    wqk = np.concatenate([wqk, wftail], axis=2)

    def gu_tiles(g, u):
        gT = g.T.reshape(CI, 128, FCN, 128).transpose(2, 1, 0, 3)
        uT = u.T.reshape(CI, 128, FCN, 128).transpose(2, 1, 0, 3)
        return np.stack([gT, uT], axis=2).reshape(FCN, 128, 2 * CI * 128)

    wgu_list = [gu_tiles(wg[e], wu[e]) for e in range(8)]
    wgu_list.append(gu_tiles(sg[0:1408], su[0:1408]))
    wgu_list.append(gu_tiles(sg[1408:2816], su[1408:2816]))
    wgu_list = [np.concatenate([g, g[0:1]], axis=0) for g in wgu_list]
    wgu_all = np.concatenate(wgu_list, axis=0)
    wgu_all = np.concatenate([wgu_all, np.zeros((120, 128, 256), dtype=f32)], axis=2)
    wpr_t = asn(np.concatenate([wqk, wgu_all], axis=0))

    def big_tiles(W):
        return W.T.reshape(CI, 128, 4, 512).transpose(2, 1, 0, 3).reshape(4, 128, CI * 512)

    wvo_t = asn(np.concatenate([big_tiles(Wv), big_tiles(Wo)], axis=0))

    def d_tiles(W):
        return W.T.reshape(FCN, 128, OBN, 512).transpose(2, 1, 0, 3).reshape(OBN, 128, FCN * 512)

    wd_list = [d_tiles(wd[e]) for e in range(8)]
    wd_list.append(d_tiles(sd[:, 0:1408]))
    wd_list.append(d_tiles(sd[:, 1408:2816]))
    wd_t = asn(np.concatenate(wd_list, axis=0))

    gw_t = asn(gwt.T.reshape(CI, 128, 8).transpose(1, 0, 2).reshape(128, CI * 8))
    ident = np.eye(128, dtype=ml_dtypes.bfloat16)
    maskut = np.tile(np.triu(np.ones((128, 128))), (1, 2)).astype(ml_dtypes.bfloat16)
    cva = np.tile(np.asarray([CONST_VALS], dtype=np.float32), (128, 1))

    return dict(wvo_t=wvo_t, wpr_t=wpr_t, wd_t=wd_t, gw_t=gw_t,
                ident=ident, maskut=maskut, cvals=cva)


_NC_CACHE = [None]


def kernel(**inputs):
    x = np.ascontiguousarray(np.asarray(inputs["L0_attn_hidden_states"], dtype=np.float32))
    w = _stage_weights(inputs)
    if _NC_CACHE[0] is None:
        _NC_CACHE[0] = build_nc()
    nc = _NC_CACHE[0]
    in_maps = []
    for core in range(8):
        b, half = core // 2, core % 2
        m = dict(w)
        m["x_own"] = np.ascontiguousarray(x[b, half * TOK:(half + 1) * TOK, :])
        m["x_pre"] = np.ascontiguousarray(x[b, 0:TOK, :])
        m["flagd"] = np.full((128, 1), float(half), dtype=np.float32)
        in_maps.append(m)
    res = run_bass_kernel_spmd(nc, in_maps, core_ids=list(range(8)))
    outp = np.empty((B, Lq, DM), dtype=np.float32)
    for core in range(8):
        b, half = core // 2, core % 2
        outp[b, half * TOK:(half + 1) * TOK, :] = np.asarray(res.results[core]["out"])
    return outp


# version 06
# speedup vs baseline: 1.2608x; 1.0397x over previous
import numpy as np
import ml_dtypes
import contextlib
import concourse.bass as bass
import concourse.mybir as mybir
from concourse.bass import ds
from concourse.bass_utils import run_bass_kernel_spmd
from concourse.alu_op_type import AluOpType as ALU

F32 = mybir.dt.float32
BF16 = mybir.dt.bfloat16
AF = mybir.ActivationFunctionType
AX = mybir.AxisListType
ET = mybir.EngineType

B, Lq, DM = 4, 4096, 2048
TOK, SEG = 2048, 1024
NSEG, NPRE = 2, 2
TCN, TBN, CI = 8, 2, 16
H, HD, FD = 32, 64, 64
NE, FCN, OBN = 10, 11, 4
EPS_LN, EPS_AT = 1e-5, 1e-12
CONST_VALS = [EPS_LN, -0.5, 1.0 / DM, -1.0, EPS_AT, -1e30, 1e-20, 2.0]

ENAMES = {"pe": ET.PE, "dv": ET.DVE, "sc": ET.Activation, "gp": ET.Pool, "sp": ET.SP}
_DRY = [False]


class _MockInst:
    def then_inc(self, *a, **k):
        return self


_MOCK = _MockInst()


class Eng:
    def __init__(self, real):
        self._real = real

    def __getattr__(self, name):
        if _DRY[0]:
            if name == "snap":
                return lambda *a, **k: 0
            return lambda *a, **k: _MOCK
        return getattr(self._real, name)


class S:
    def __init__(self, nc, es, name):
        self.h = es.enter_context(nc.semaphore(name))
        self.n = 0
        self.name = name

    def inc(self, inst, k=1):
        self.n += k
        if inst is not None and not isinstance(inst, _MockInst):
            inst.then_inc(self.h, k)
        return self.n


class KB:
    def __init__(self, nc, es):
        self.nc = nc
        self.es = es
        self.sems = []
        attr = {"pe": "tensor", "dv": "vector", "sc": "scalar", "gp": "gpsimd", "sp": "sync"}
        self.eng = {n: Eng(getattr(nc, a)) for n, a in attr.items()}
        self.real = {n: getattr(nc, a) for n, a in attr.items()}
        self.rW = {}
        self.tmp = {}

    def sem(self, name):
        s = S(self.nc, self.es, name)
        self.sems.append(s)
        return s

    def declare_waiters(self, pairs):
        for en, sem in pairs:
            r = self.real[en].alloc_register(f"rW_{en}_{sem.name}")
            self.real[en].reg_mov(r, 0)
            self.rW[(en, sem)] = r
        for en in ENAMES:
            t = self.real[en].alloc_register(f"tmp_{en}")
            self.tmp[en] = t

    def wait(self, en, sem, off=0):
        self.waitabs(en, sem, sem.n + off)

    def waitabs(self, en, sem, static):
        if _DRY[0]:
            return
        e = self.real[en]
        key = (en, sem)
        if key in self.rW:
            e.reg_alu(self.tmp[en], self.rW[key], static, mybir.AluOpType.add)
            e.wait_ge(sem.h, self.tmp[en])
        else:
            assert static >= 0
            e.wait_ge(sem.h, static)

    def forloop(self, n, engines, body, extra=None):
        extra = extra or {}
        snap = [s.n for s in self.sems]
        if _DRY[0]:
            body()
            for s, s0 in zip(self.sems, snap):
                s.n = s0 + n * (s.n - s0)
            return
        _DRY[0] = True
        body()
        real_d = {s: s.n - s0 for s, s0 in zip(self.sems, snap)}
        for s, s0 in zip(self.sems, snap):
            s.n = s0
        _DRY[0] = False
        deltas = dict(real_d)
        for s, d in extra.items():
            deltas[s] = deltas.get(s, 0) + d

        nc = self.nc
        engs = [ENAMES[e] for e in engines]
        loop_id = nc.next_id()
        ls, le = f"kf_{loop_id}_loop", f"kf_{loop_id}_end"
        registers = nc.alloc_registers(f"kf_{loop_id}_i", engines=engs)
        nc.regs_mov(registers, 0)
        nc.br(ls, engines=engs)
        with nc.body(ls, valid_engines=engs):
            body()
            for en in engines:
                for (wen, sem), r in self.rW.items():
                    if wen == en and deltas.get(sem, 0):
                        self.real[en].reg_alu(r, r, deltas[sem], mybir.AluOpType.add)
            nc.regs_alu(registers, registers, 1, op=mybir.AluOpType.add)
            nc.br_lt(registers, n, on_true=ls, on_false=le, engines=engs)
        nc.switch_bb(le)
        for e, h in zip(registers.engines, registers.handles):
            nc.engines[e].free_register(h)
        for en in engines:
            for (wen, sem), r in self.rW.items():
                if wen == en and deltas.get(sem, 0):
                    self.real[en].reg_alu(r, r, -n * deltas[sem], mybir.AluOpType.add)
        for s, s0 in zip(self.sems, snap):
            s.n = s0 + n * real_d.get(s, 0)

    def barrier(self):
        if not _DRY[0]:
            self.nc.all_engine_barrier()


def build_nc(parts="PABCRDS"):
    nc = bass.Bass("TRN2", target_bir_lowering=False, debug=False)
    dt = nc.dram_tensor
    x_own = dt("x_own", [TOK, DM], F32, kind="ExternalInput").ap()
    x_pre = dt("x_pre", [TOK, DM], F32, kind="ExternalInput").ap()
    flagd = dt("flagd", [128, 1], F32, kind="ExternalInput").ap()
    wvo_t = dt("wvo_t", [8, 128, CI * 512], F32, kind="ExternalInput").ap()
    wpr_t = dt("wpr_t", [16 + NE * 12, 128, 4352], F32, kind="ExternalInput").ap()
    wd_t = dt("wd_t", [NE * OBN, 128, FCN * 512], F32, kind="ExternalInput").ap()
    gw_t = dt("gw_t", [128, CI * 8], F32, kind="ExternalInput").ap()
    ident = dt("ident", [128, 128], BF16, kind="ExternalInput").ap()
    maskut = dt("maskut", [128, 256], BF16, kind="ExternalInput").ap()
    cvals = dt("cvals", [128, 8], F32, kind="ExternalInput").ap()
    out = dt("out", [TOK, DM], F32, kind="ExternalOutput").ap()
    dscr = dt("dscr", [NE * 1024, DM], F32, kind="Internal").ap()

    es = contextlib.ExitStack()
    sbt = lambda name, shape, dty: es.enter_context(nc.sbuf_tensor(name, shape, dty))
    pst = lambda name, shape, dty: es.enter_context(nc.psum_tensor(name, shape, dty))

    xn_fm = sbt("xn_fm", [128, CI * SEG], BF16)
    yat = sbt("yat", [128, 16 * SEG], BF16)
    h_fm = sbt("h_fm", [128, FCN * SEG], BF16)
    v_ext = sbt("v_ext", [128, 8 * TCN * 65], BF16)
    qk_fm = sbt("qk_fm", [128, 2 * SEG], BF16)
    x_sb = sbt("x_sb", [128, DM], F32)
    wvb = sbt("wvb", [128, CI * 512], BF16)
    wqg = sbt("wqg", [128, 2 * 4352], BF16)
    wdb = sbt("wdb", [128, 2 * FCN * 512], BF16)
    stg_c = sbt("stg_c", [128, 2 * 512], F32)
    stg_d = sbt("stg_d", [128, 4 * 1024], F32)
    gw_sb = sbt("gw_sb", [128, CI * 8], BF16)
    cw_sb = sbt("cw_sb", [128, NE * TCN], F32)
    cwcur = sbt("cwcur", [128, 2 * TCN], F32)
    S32 = sbt("S32", [64, H * 65], F32)
    S_cur = sbt("S_cur", [64, 8 * 65], F32)
    S16c = sbt("S16c", [64, 130], BF16)
    y_cur = sbt("y_cur", [128, 4 * SEG], BF16)
    sg_sb = sbt("sg_sb", [128, 512], F32)
    id_sb = sbt("id_sb", [128, 128], BF16)
    mk2 = sbt("mk2", [128, 256], BF16)
    ones64 = sbt("ones64", [1, 64], F32)
    flag_sb = sbt("flag_sb", [128, 1], F32)
    cv_sb = sbt("cv_sb", [128, 8], F32)
    ex_k = sbt("ex_k", [128, 128], BF16)
    fq_tm = sbt("fq_tm", [128, 128], BF16)
    fk_tm = sbt("fk_tm", [128, 128], BF16)
    fqkT = sbt("fqkT", [64, 512], BF16)
    atm = sbt("atm", [128, 256], BF16)
    rd_f = sbt("rd_f", [1, 256], F32)
    rden = sbt("rden", [64, 256], F32)
    colA = sbt("colA", [128, 1], F32)
    colB = sbt("colB", [128, 1], F32)
    colC = sbt("colC", [128, 1], F32)
    colD = sbt("colD", [128, 1], F32)
    r_s = sbt("r_s", [128, 8], F32)
    r_s2 = sbt("r_s2", [128, 8], F32)

    for i, v in enumerate(CONST_VALS):
        nc.const_aps.register(F32, v, cv_sb[:, i:i + 1])

    P0 = pst("P0", [128, 512], F32)
    P1 = pst("P1", [128, 512], F32)
    P2 = pst("P2", [128, 512], F32)
    P3 = pst("P3", [128, 512], F32)
    P4 = pst("P4", [128, 512], F32)
    P5 = pst("P5", [128, 512], F32)
    P6 = pst("P6", [64, 512], F32)
    P7 = pst("P7", [128, 1024], BF16)

    k = KB(nc, es)
    LV = k.sem("LV")
    LQ0 = k.sem("LQ0")
    LQ1 = k.sem("LQ1")
    LQs = [LQ0, LQ1]
    LD0 = k.sem("LD0")
    LD1 = k.sem("LD1")
    LDs = [LD0, LD1]
    LXA = k.sem("LXA")
    FXA = k.sem("FXA")
    FWV = k.sem("FWV")
    FWQ = k.sem("FWQ")
    FWD = k.sem("FWD")
    PEm = k.sem("PEm")
    PEn = k.sem("PEn")
    DVm = k.sem("DVm")
    SCm = k.sem("SCm")
    STC0 = k.sem("STC0")
    STC1 = k.sem("STC1")
    STCs = [STC0, STC1]
    STD = k.sem("STD")
    SPD = k.sem("SPD")
    SGC = k.sem("SGC")
    SGD = k.sem("SGD")
    LC = k.sem("LC")

    k.declare_waiters([
        ("pe", LV), ("pe", LQ0), ("pe", LQ1), ("pe", LD0), ("pe", LD1),
        ("pe", DVm), ("pe", SCm), ("pe", FXA), ("pe", SGC), ("pe", SGD),
        ("dv", PEm), ("dv", PEn), ("dv", SCm), ("dv", LXA), ("dv", SPD), ("dv", FWD), ("dv", DVm),
        ("dv", FWV), ("dv", FWQ),
        ("sc", DVm), ("sc", PEm), ("sc", LXA), ("sc", STC0), ("sc", STC1), ("sc", SCm), ("sc", FWV),
        ("gp", FWV), ("gp", FWQ), ("gp", FWD), ("gp", SPD), ("gp", STD), ("gp", STC0), ("gp", STC1),
        ("sp", FXA), ("sp", SGC), ("sp", SGD), ("sp", SPD),
    ])
    pe, dv, sc, gp, sp = (k.eng[n] for n in ("pe", "dv", "sc", "gp", "sp"))
    PE_, DV_, SC_, GP_, SP_ = (k.real[n] for n in ("pe", "dv", "sc", "gp", "sp"))

    r_uvo = GP_.alloc_register("r_uvo")
    r_uqg = GP_.alloc_register("r_uqg")
    r_uwd = GP_.alloc_register("r_uwd")
    r_so = GP_.alloc_register("r_so")
    r_xrow = SP_.alloc_register("r_xrow")
    r_cst = SP_.alloc_register("r_cst")
    r_sd = SP_.alloc_register("r_sd")
    r_tcA = DV_.alloc_register("r_tcA")
    r_cw = DV_.alloc_register("r_cw")
    r_g8s = SC_.alloc_register("r_g8s")
    r_g8y = SC_.alloc_register("r_g8y")

    wvo_th = wvo_t.tensor
    wpr_th = wpr_t.tensor
    wd_th = wd_t.tensor
    xn4 = xn_fm[:].rearrange("p (c t) -> p c t", c=CI)
    p7v = P7[:].rearrange("p (j e) -> p j e", j=8)
    vve = v_ext[:].rearrange("p (a b c) -> p a b c", a=8, b=TCN, c=65)
    cwv = cw_sb[:].rearrange("p (a b) -> p a b", a=NE, b=TCN)

    with es:
        LC.inc(gp.dma_start(out=cv_sb[:], in_=cvals), 16)
        LC.inc(gp.dma_start(out=id_sb[:], in_=ident), 16)
        LC.inc(gp.dma_start(out=mk2[:], in_=maskut), 16)
        LC.inc(gp.dma_start(out=gw_sb[:], in_=gw_t), 16)
        LC.inc(gp.dma_start(out=flag_sb[:], in_=flagd), 16)
        sp.sem_inc(FWV.h, 1)
        FWV.n += 1
        sp.sem_inc(FWQ.h, 2)
        FWQ.n += 2
        sp.sem_inc(FWD.h, 2)
        FWD.n += 2

        sp.sem_inc(SGC.h, 2)
        SGC.n += 2
        dv.memset(S32[:], 0.0)
        dv.memset(vve[:, :, :, 64:65], 1.0)
        dv.memset(ones64[:], 1.0)
        dv.memset(cwv[:, 8:10, :], 1.0)
        for r in (r_uvo, r_uqg, r_uwd, r_so):
            gp.reg_mov(r, 0)
        for r in (r_xrow, r_cst, r_sd):
            sp.reg_mov(r, 0)
        for r in (r_tcA, r_cw):
            dv.reg_mov(r, 0)
        for r in (r_g8s, r_g8y):
            sc.reg_mov(r, 0)
        k.barrier()

        def pool_stream(n, src_th, elen, reg, sem_loads, sem_free, dests):

            def body():
                u0 = sum(s.n for s in sem_loads) // 16
                for j, (dst, sl) in enumerate(zip(dests, sem_loads)):
                    k.waitabs("gp", sem_free, u0 + j + 1)
                    ap = bass.AP(src_th, reg, [[elen, 128], [1, elen]])
                    sl.inc(gp.dma_start(out=dst, in_=ap), 16)
                    gp.reg_add(reg, reg, 128 * elen)

            k.forloop(n, ["gp"], body, extra={sem_free: len(dests)})

        def poolB_stream():

            def body():
                uv0 = LV.n // 16
                k.waitabs("gp", FWV, uv0 + 1)
                apv = bass.AP(wvo_th, r_uvo, [[CI * 512, 128], [1, CI * 512]])
                LV.inc(gp.dma_start(out=wvb[:], in_=apv), 16)
                gp.reg_add(r_uvo, r_uvo, 128 * CI * 512)
                uq0 = (LQ0.n + LQ1.n) // 16

                def qpair():
                    for j in (0, 1):
                        k.waitabs("gp", FWQ, uq0 + j + 1)
                        apq = bass.AP(wpr_th, r_uqg, [[4352, 128], [1, 4352]])
                        LQs[j].inc(gp.dma_start(out=wqg[:, j * 4352:(j + 1) * 4352], in_=apq), 16)
                        gp.reg_add(r_uqg, r_uqg, 128 * 4352)

                k.forloop(2, ["gp"], qpair, extra={FWQ: 2})

            k.forloop(4, ["gp"], body, extra={FWV: 1, FWQ: 4})


        def phaseA(xsrc):
            def body():
                k.wait("sp", FXA, 0)
                rr = sp.snap(r_xrow, min_val=0, max_val=TOK - 128)
                lx = LXA.inc(sp.dma_start(out=x_sb[:], in_=xsrc[ds(rr, 128), :]), 16)
                sp.reg_add(r_xrow, r_xrow, 128)
                k.waitabs("sc", LXA, lx)
                sq = SCm.inc(sc.activation(h_fm[:, 0:2048], x_sb[:], AF.Square, accum_out=colA[:]))
                k.waitabs("dv", LXA, lx)
                DVm.inc(dv.tensor_reduce(colB[:], x_sb[:], AX.X, ALU.add))
                k.wait("dv", DVm, 0)
                DVm.inc(dv.tensor_scalar_mul(colB[:], colB[:], 1.0 / DM))
                k.wait("dv", DVm, 0)
                k.waitabs("dv", SCm, sq)
                dv.tensor_scalar_mul(colC[:], colA[:], 1.0 / DM)
                DVm.inc(dv.tensor_tensor(colD[:], colB[:], colB[:], ALU.mult))
                k.wait("dv", DVm, 0)
                d3 = DVm.inc(dv.tensor_tensor(colC[:], colC[:], colD[:], ALU.subtract))
                k.waitabs("sc", DVm, d3)
                s2 = SCm.inc(sc.activation(colC[:], colC[:], AF.Ln, bias=EPS_LN))
                k.waitabs("sc", SCm, s2)
                s3 = SCm.inc(sc.activation(colC[:], colC[:], AF.Exp, scale=-0.5))
                k.waitabs("dv", SCm, s3)
                d4 = DVm.inc(dv.tensor_scalar(colD[:], colB[:], colC[:], -1.0,
                                              op0=ALU.mult, op1=ALU.mult))
                k.waitabs("sc", DVm, d4)
                k.wait("sc", PEm, 0)
                fx = FXA.inc(sc.activation(h_fm[:, 2048:4096], x_sb[:], AF.Identity,
                                           bias=colD[:], scale=colC[:]))
                toff = dv.snap(r_tcA, min_val=0, max_val=SEG - 128)
                for half in (0, 1):
                    k.waitabs("pe", FXA, fx)
                    k.wait("pe", DVm, 0)
                    inst = None
                    for j in range(8):
                        c = half * 8 + j
                        inst = pe.transpose(P7[:, j * 128:(j + 1) * 128],
                                            h_fm[:, 2048 + c * 128:2048 + (c + 1) * 128],
                                            id_sb[:])
                    p1 = PEm.inc(inst)
                    k.waitabs("dv", PEm, p1)
                    DVm.inc(dv.tensor_copy(xn4[:, half * 8:(half + 1) * 8, ds(toff, 128)],
                                           p7v[:, :, :]))
                dv.reg_add(r_tcA, r_tcA, 128)

            k.forloop(TCN, ["pe", "dv", "sc", "sp"], body)
            if not _DRY[0]:
                DV_.reg_add(r_tcA, r_tcA, -SEG)

        def vproj(lv_abs):
            for tc in range(TCN):
                bank = P0 if tc % 2 == 0 else P1
                bk4 = bank[:].rearrange("p (a b) -> p a b", a=8)
                k.waitabs("pe", LV, lv_abs)
                k.wait("pe", DVm, 0)
                for c in range(CI):
                    inst = pe.matmul(bank[:, 0:512],
                                     xn_fm[:, c * SEG + tc * 128:c * SEG + (tc + 1) * 128],
                                     wvb[:, c * 512:(c + 1) * 512],
                                     start=(c == 0), stop=(c == CI - 1))
                if tc == TCN - 1:
                    fv = FWV.inc(inst)
                    k.waitabs("dv", FWV, fv)
                else:
                    p1 = PEm.inc(inst)
                    k.waitabs("dv", PEm, p1)
                DVm.inc(dv.tensor_copy(vve[:, :, tc, 0:64], bk4[:, :, :]))

        def qkproj(lq_abs, slot, prime):
            ms = (1,) if prime else (0, 1)
            for m in ms:
                for tb in range(TBN):
                    bank = P0 if tb == 0 else P1
                    k.waitabs("pe", LQs[slot], lq_abs)
                    k.wait("pe", DVm, 0)
                    for c in range(CI):
                        inst = pe.matmul(bank[:, 0:512],
                                         wqg[:, slot * 4352 + m * 2048 + c * 128:
                                             slot * 4352 + m * 2048 + (c + 1) * 128],
                                         xn_fm[:, c * SEG + tb * 512:c * SEG + (tb + 1) * 512],
                                         start=(c == 0), stop=(c == CI - 1))
                    p1 = PEm.inc(inst)
                    k.waitabs("dv", PEm, p1)
                    DVm.inc(dv.tensor_copy(qk_fm[:, m * SEG + tb * 512:m * SEG + (tb + 1) * 512],
                                           bank[:, 0:512]))

        def mixer_pair(hp, prime):
            hA, hB = 2 * (hp % 4), 2 * (hp % 4) + 1
            slot = hp % 2
            wfq_A = wqg[0:64, slot * 4352 + 4096:slot * 4352 + 4096 + 64]
            wfq_B = wqg[64:128, slot * 4352 + 4160:slot * 4352 + 4160 + 64]
            wfk_A = wqg[0:64, slot * 4352 + 4224:slot * 4352 + 4224 + 64]
            wfk_B = wqg[64:128, slot * 4352 + 4288:slot * 4352 + 4288 + 64]
            sP = S_cur[0:64, hA * 65:hA * 65 + 130]
            if not prime:
                k.wait("dv", SCm, 0)
                k.wait("dv", DVm, 0)
                DVm.inc(dv.tensor_copy(S16c[:], sP))
            for c in range(TCN):
                if c == 0:
                    k.wait("pe", DVm, 0)
                    k.wait("pe", SCm, 0)
                else:
                    k.wait("pe", DVm, -4 if prime else -10)
                    k.wait("pe", SCm, 0 if prime else -1)
                if not prime:
                    pe.matmul(P4[:, 0:64], qk_fm[0:64, c * 128:(c + 1) * 128],
                              wfq_A, start=True, stop=True)
                pe.matmul(P4[:, 64:128], qk_fm[0:64, SEG + c * 128:SEG + (c + 1) * 128],
                          wfk_A, start=True, stop=True)
                if not prime:
                    pe.matmul(P3[:, 256:320], qk_fm[64:128, c * 128:(c + 1) * 128],
                              wfq_B, start=True, stop=True)
                p1 = PEm.inc(pe.matmul(P3[:, 320:384],
                                       qk_fm[64:128, SEG + c * 128:SEG + (c + 1) * 128],
                                       wfk_B, start=True, stop=True))
                k.waitabs("dv", PEm, p1)
                if prime:
                    dv.tensor_reduce(r_s[:, 1:2], P4[:, 64:128], AX.X, ALU.max, negate=True)
                    d1 = DVm.inc(dv.tensor_reduce(r_s[:, 3:4], P3[:, 320:384],
                                                  AX.X, ALU.max, negate=True))
                else:
                    dv.tensor_reduce(
                        r_s[:, 0:2], P4[:, 0:128].rearrange("p (a b) -> p a b", b=64),
                        AX.X, ALU.max, negate=True)
                    d1 = DVm.inc(dv.tensor_reduce(
                        r_s[:, 2:4], P3[:, 256:384].rearrange("p (a b) -> p a b", b=64),
                        AX.X, ALU.max, negate=True))
                k.waitabs("sc", DVm, d1)
                if not prime:
                    sc.activation(fq_tm[:, 0:64], P4[:, 0:64], AF.Exp, bias=r_s[:, 0:1])
                    s0 = SCm.inc(sc.activation(fq_tm[:, 64:128], P3[:, 256:320],
                                               AF.Exp, bias=r_s[:, 2:3]))
                sc.activation(ex_k[:, 0:64], P4[:, 64:128], AF.Exp, bias=r_s[:, 1:2])
                s1 = SCm.inc(sc.activation(ex_k[:, 64:128], P3[:, 320:384],
                                           AF.Exp, bias=r_s[:, 3:4]))
                k.waitabs("dv", SCm, s1)
                DVm.inc(dv.tensor_reduce(
                    r_s2[:, 0:2], ex_k[:].rearrange("p (a b) -> p a b", b=64),
                    AX.X, ALU.add))
                k.wait("dv", DVm, 0)
                DVm.inc(dv.reciprocal(r_s2[:, 0:2], r_s2[:, 0:2]))
                k.wait("dv", DVm, 0)
                if prime:
                    dv.tensor_scalar(fk_tm[:, 0:64], ex_k[:, 0:64], r_s2[:, 0:1],
                                     flag_sb[:], op0=ALU.mult, op1=ALU.mult)
                    dfk = DVm.inc(dv.tensor_scalar(fk_tm[:, 64:128], ex_k[:, 64:128],
                                                   r_s2[:, 1:2], flag_sb[:],
                                                   op0=ALU.mult, op1=ALU.mult))
                else:
                    dv.tensor_scalar_mul(fk_tm[:, 0:64], ex_k[:, 0:64], r_s2[:, 0:1])
                    dfk = DVm.inc(dv.tensor_scalar_mul(fk_tm[:, 64:128], ex_k[:, 64:128],
                                                       r_s2[:, 1:2]))

                if not prime:
                    k.waitabs("pe", SCm, s0)
                    pe.transpose(P7[0:64, 0:128], fq_tm[:, 0:64], id_sb[:])
                    pe.transpose(P7[0:64, 128:256], fq_tm[:, 64:128], id_sb[:])
                    k.waitabs("pe", DVm, dfk)
                    pe.transpose(P7[0:64, 256:384], fk_tm[:, 0:64], id_sb[:])
                    p2 = PEm.inc(pe.transpose(P7[0:64, 384:512], fk_tm[:, 64:128], id_sb[:]))
                    k.waitabs("sc", PEm, p2)
                    s2 = SCm.inc(sc.copy(fqkT[0:64, 0:512], P7[0:64, 0:512]))
                    k.waitabs("pe", SCm, s2)
                    pe.matmul(P3[:, 0:128], fqkT[0:64, 256:384], fqkT[0:64, 0:128],
                              start=True, stop=True)
                    p3 = PEm.inc(pe.matmul(P3[:, 128:256], fqkT[0:64, 384:512],
                                           fqkT[0:64, 128:256], start=True, stop=True))
                    k.waitabs("dv", PEm, p3)
                    d4 = DVm.inc(dv.tensor_tensor(atm[:, 0:256], P3[:, 0:256],
                                                  mk2[:, 0:256], ALU.mult))
                    k.waitabs("pe", DVm, d4)
                    pe.matmul(P5[0:64, 256:384],
                              v_ext[:, (hA * TCN + c) * 65:(hA * TCN + c) * 65 + 64],
                              atm[:, 0:128], start=True, stop=False)
                    pe.matmul(P5[0:64, 256:384], S16c[0:64, 0:64], fqkT[0:64, 0:128],
                              start=False, stop=True)
                    pe.matmul(P5[0:64, 384:512],
                              v_ext[:, (hB * TCN + c) * 65:(hB * TCN + c) * 65 + 64],
                              atm[:, 128:256], start=True, stop=False)
                    pe.matmul(P5[0:64, 384:512], S16c[0:64, 65:129], fqkT[0:64, 128:256],
                              start=False, stop=True)
                    pe.matmul(P2[0:1, 0:128], mk2[:, 127:128], atm[:, 0:128],
                              start=True, stop=False)
                    pe.matmul(P2[0:1, 0:128], S16c[0:64, 64:65], fqkT[0:64, 0:128],
                              start=False, stop=True)
                    pe.matmul(P2[0:1, 128:256], mk2[:, 127:128], atm[:, 128:256],
                              start=True, stop=False)
                    p4 = PEm.inc(pe.matmul(P2[0:1, 128:256], S16c[0:64, 129:130],
                                           fqkT[0:64, 128:256], start=False, stop=True))
                    k.waitabs("dv", PEm, p4)
                    DVm.inc(dv.tensor_scalar_add(rd_f[:], P2[0:1, 0:256], EPS_AT))
                    k.wait("dv", DVm, 0)
                    d6 = DVm.inc(dv.reciprocal(rd_f[:], rd_f[:]))
                    k.waitabs("pe", DVm, d6)
                    pe.matmul(P2[0:64, 256:384], ones64[:], rd_f[0:1, 0:128],
                              start=True, stop=True)
                    p5 = PEm.inc(pe.matmul(P2[0:64, 384:512], ones64[:], rd_f[0:1, 128:256],
                                           start=True, stop=True))
                    k.waitabs("dv", PEm, p5)
                    DVm.inc(dv.tensor_copy(rden[:, 0:256], P2[0:64, 256:512]))
                    k.wait("dv", DVm, 0)
                    k.wait("dv", SCm, 0)
                    cblk = (hp % 4) * SEG + c * 128
                    dv.tensor_tensor(y_cur[0:64, cblk:cblk + 128],
                                     P5[0:64, 256:384], rden[:, 0:128], ALU.mult)
                    DVm.inc(dv.tensor_tensor(y_cur[64:128, cblk:cblk + 128],
                                             P5[0:64, 384:512], rden[:, 128:256], ALU.mult))
                k.waitabs("pe", DVm, dfk)
                pe.matmul(P6[0:64, 0:65], fk_tm[:, 0:64],
                          v_ext[:, (hA * TCN + c) * 65:(hA * TCN + c + 1) * 65],
                          start=True, stop=True)
                inst6 = pe.matmul(P6[0:64, 65:130], fk_tm[:, 64:128],
                                  v_ext[:, (hB * TCN + c) * 65:(hB * TCN + c + 1) * 65],
                                  start=True, stop=True)
                if c == TCN - 1:
                    f6 = FWQ.inc(inst6)
                    k.waitabs("dv", FWQ, f6)
                else:
                    p6 = PEm.inc(inst6)
                    k.waitabs("dv", PEm, p6)
                k.wait("dv", SCm, 0)
                d8 = DVm.inc(dv.tensor_tensor(sP, sP, P6[0:64, 0:130], ALU.add))
                if not prime:
                    k.waitabs("dv", DVm, d8)
                    DVm.inc(dv.tensor_copy(S16c[:], sP))

        def phaseB(prime, lv_base, lq_base):
            def body():
                k.wait("sc", PEm, 0)
                k.wait("sc", DVm, 0)
                k.wait("sc", SCm, 0)
                ssnap = sc.snap(r_g8s, min_val=0, max_val=H * 65 - 520)
                s32slice = S32[0:64, ds(ssnap, 520)]
                SCm.inc(sc.copy(S_cur[:], s32slice))
                vproj(lv_base + 16)
                for hp in range(4):
                    qkproj(lq_base[hp % 2] + 16 * (hp // 2 + 1), hp % 2, prime)
                    mixer_pair(hp, prime)
                k.wait("sc", DVm, 0)
                SCm.inc(sc.copy(s32slice, S_cur[:]))
                if not prime:
                    ysnap = sc.snap(r_g8y, min_val=0, max_val=12288)
                    SCm.inc(sc.copy(yat[:, ds(ysnap, 4096)], y_cur[:]))
                sc.reg_add(r_g8s, r_g8s, 520)
                sc.reg_add(r_g8y, r_g8y, 4096)

            k.forloop(4, ["pe", "dv", "sc"], body, extra={LV: 16, LQ0: 32, LQ1: 32})
            if not _DRY[0]:
                SC_.reg_add(r_g8s, r_g8s, -2080)
                SC_.reg_add(r_g8y, r_g8y, -16384)

        def phaseC(lv_base):
            def body():
                for tc in range(TCN):
                    par = tc % 2
                    bank = P0 if par == 0 else P1
                    k.waitabs("pe", LV, lv_base + 16 * 5)
                    k.wait("pe", SCm, 0)
                    k.wait("pe", SGC, -1)
                    k.wait("pe", DVm, 0)
                    for c in range(CI):
                        inst = pe.matmul(bank[:, 0:512],
                                         yat[:, c * SEG + tc * 128:c * SEG + (tc + 1) * 128],
                                         wvb[:, c * 512:(c + 1) * 512],
                                         start=(c == 0), stop=(c == CI - 1))
                    if tc == TCN - 1:
                        fv = FWV.inc(inst)
                        k.waitabs("sc", FWV, fv)
                    else:
                        p1 = PEm.inc(inst)
                        k.waitabs("sc", PEm, p1)
                    k.wait("sc", STCs[par], 0)
                    sg = SGC.inc(sc.copy(stg_c[:, par * 512:(par + 1) * 512], bank[:, 0:512]))
                    k.waitabs("sp", SGC, sg)
                    STCs[par].inc(sp.dma_start(
                        out=bass.AP(out.tensor, r_cst, [[DM, 128], [1, 512]]),
                        in_=stg_c[:, par * 512:(par + 1) * 512]), 16)
                    sp.reg_add(r_cst, r_cst, 128 * DM)
                sp.reg_add(r_cst, r_cst, 512 - TCN * 128 * DM)

            k.forloop(OBN, ["pe", "sc", "sp"], body, extra={LV: 16})

        def router():
            for tc in range(TCN):
                k.wait("pe", DVm, 0)
                k.wait("pe", SCm, 0)
                for c in range(CI):
                    inst = pe.matmul(P2[:, 0:8],
                                     xn_fm[:, c * SEG + tc * 128:c * SEG + (tc + 1) * 128],
                                     gw_sb[:, c * 8:(c + 1) * 8],
                                     start=(c == 0), stop=(c == CI - 1))
                p1 = PEm.inc(inst)
                k.waitabs("dv", PEm, p1)
                d1 = DVm.inc(dv.tensor_reduce(colA[:], P2[:, 0:8], AX.X, ALU.max, negate=True))
                k.waitabs("sc", DVm, d1)
                s1 = SCm.inc(sc.activation(r_s[:], P2[:, 0:8], AF.Exp, bias=colA[:]))
                k.waitabs("dv", SCm, s1)
                DVm.inc(dv.tensor_reduce(colB[:], r_s[:], AX.X, ALU.add))
                k.wait("dv", DVm, 0)
                DVm.inc(dv.reciprocal(colB[:], colB[:]))
                k.wait("dv", DVm, 0)
                DVm.inc(dv.tensor_scalar_mul(r_s[:], r_s[:], colB[:]))
                k.wait("dv", DVm, 0)
                DVm.inc(dv.tensor_reduce(colA[:], r_s[:], AX.X, ALU.max))
                k.wait("dv", DVm, 0)
                DVm.inc(dv.tensor_scalar(r_s2[:], r_s[:], colA[:], None, op0=ALU.is_ge))
                k.wait("dv", DVm, 0)
                DVm.inc(dv.scalar_tensor_tensor(r_s2[:], r_s2[:], -1e30, r_s[:],
                                                op0=ALU.mult, op1=ALU.add))
                k.wait("dv", DVm, 0)
                DVm.inc(dv.tensor_reduce(colB[:], r_s2[:], AX.X, ALU.max))
                k.wait("dv", DVm, 0)
                DVm.inc(dv.tensor_scalar(r_s2[:], r_s[:], colB[:], None, op0=ALU.is_ge))
                k.wait("dv", DVm, 0)
                DVm.inc(dv.tensor_tensor(r_s2[:], r_s2[:], r_s[:], ALU.mult))
                k.wait("dv", DVm, 0)
                DVm.inc(dv.tensor_reduce(colC[:], r_s2[:], AX.X, ALU.add))
                k.wait("dv", DVm, 0)
                DVm.inc(dv.tensor_scalar_add(colC[:], colC[:], 1e-20))
                k.wait("dv", DVm, 0)
                DVm.inc(dv.reciprocal(colC[:], colC[:]))
                k.wait("dv", DVm, 0)
                DVm.inc(dv.tensor_scalar_mul(cwv[:, 0:8, tc:tc + 1], r_s2[:], colC[:]))

        def phaseD(lq_base, ld_base, spd_base):
            def body():
                for eh in (0, 1):
                    k.wait("dv", DVm, 0)
                    cwsnap = dv.snap(r_cw, min_val=0, max_val=NE * TCN - 8)
                    DVm.inc(dv.tensor_copy(cwcur[:, eh * 8:(eh + 1) * 8], cw_sb[:, ds(cwsnap, 8)]))
                    dv.reg_add(r_cw, r_cw, 8)
                    for fc in range(FCN):
                        slot = fc % 2
                        gbank = P0 if slot == 0 else P1
                        ubank = P2 if slot == 0 else P3
                        for tb in range(TBN):
                            k.waitabs("pe", LQs[fc % 2],
                                      lq_base[fc % 2] + 16 * (8 + eh * 6 + fc // 2 + 1))
                            if fc >= 2 and tb == 0:
                                k.wait("pe", SCm, -2)
                                k.wait("pe", DVm, -2)
                            else:
                                k.wait("pe", SCm, 0)
                                k.wait("pe", DVm, 0)
                            if fc == 0 and tb == 0:
                                k.wait("pe", SGC, 0)
                            for c in range(CI):
                                ig = pe.matmul(gbank[:, 0:512],
                                               wqg[:, slot * 4352 + c * 128:slot * 4352 + (c + 1) * 128],
                                               xn_fm[:, c * SEG + tb * 512:c * SEG + (tb + 1) * 512],
                                               start=(c == 0), stop=(c == CI - 1))
                            pg = PEm.inc(ig)
                            for c in range(CI):
                                iu = pe.matmul(ubank[:, 0:512],
                                               wqg[:, slot * 4352 + 2048 + c * 128:
                                                   slot * 4352 + 2048 + (c + 1) * 128],
                                               xn_fm[:, c * SEG + tb * 512:c * SEG + (tb + 1) * 512],
                                               start=(c == 0), stop=(c == CI - 1))
                            if tb == TBN - 1:
                                pu = ("FWQ", FWQ.inc(iu))
                            else:
                                pu = ("PEn", PEn.inc(iu))
                            k.waitabs("sc", PEm, pg)
                            k.wait("sc", DVm, 0)
                            s1 = SCm.inc(sc.activation(sg_sb[:, 0:512],
                                                       gbank[:, 0:512], AF.Silu))
                            k.waitabs("dv", SCm, s1)
                            k.waitabs("dv", FWQ if pu[0] == "FWQ" else PEn, pu[1])
                            if fc == 0 and tb == 0:
                                k.wait("dv", FWD, 0)
                            DVm.inc(dv.tensor_tensor(
                                h_fm[:, fc * SEG + tb * 512:fc * SEG + (tb + 1) * 512],
                                sg_sb[:, 0:512], ubank[:, 0:512], ALU.mult))
                    for obp in (0, 1):
                        for tc in range(TCN):
                            gidx = eh * 16 + obp * 8 + tc
                            slot4 = tc % 4
                            for oh in (0, 1):
                                ob = obp * 2 + oh
                                bank = P0 if oh == 0 else P1
                                k.waitabs("pe", LDs[ob % 2],
                                          ld_base[ob % 2] + 16 * (eh * 2 + ob // 2 + 1))
                                if tc == 0 or oh == 0:
                                    k.wait("pe", DVm, 0)
                                if tc == 0 or oh == 1:
                                    k.wait("pe", SGD, 0)
                                if obp == 0 and tc == 0 and oh == 0:
                                    k.waitabs("pe", LQ1,
                                              lq_base[1] + 16 * (8 + eh * 6 + 6))
                                first_dn = None
                                for fc in range(FCN):
                                    idn = pe.matmul(bank[:, 0:512],
                                                    h_fm[:, fc * SEG + tc * 128:fc * SEG + (tc + 1) * 128],
                                                    wdb[:, (ob % 2) * FCN * 512 + fc * 512:
                                                        (ob % 2) * FCN * 512 + (fc + 1) * 512],
                                                    start=(fc == 0), stop=(fc == FCN - 1))
                                    if fc == 0:
                                        first_dn = idn
                                if obp == 0 and tc == 0 and oh == 0:
                                    FWQ.inc(first_dn)
                                if tc == TCN - 1:
                                    fd = FWD.inc(idn)
                                    k.waitabs("dv", FWD, fd)
                                else:
                                    pd = PEm.inc(idn)
                                    k.waitabs("dv", PEm, pd)
                                k.waitabs("dv", SPD, spd_base + 16 * (gidx // 4))
                                inst = dv.tensor_scalar_mul(
                                    stg_d[:, slot4 * 1024 + oh * 512:slot4 * 1024 + (oh + 1) * 512],
                                    bank[:, 0:512],
                                    cwcur[:, eh * 8 + tc:eh * 8 + tc + 1])
                                if oh == 1:
                                    SGD.inc(inst)
                                else:
                                    DVm.inc(inst)

            k.forloop(NE // 2, ["pe", "dv", "sc"], body,
                      extra={LQ0: 192, LQ1: 192, LD0: 64, LD1: 64, SPD: 2 * 4 * 16})
            if not _DRY[0]:
                DV_.reg_add(r_cw, r_cw, -NE * TCN)

        def sp_dstores(sgd_base):
            SROW = 512 * DM

            def tcq_body():
                k.waitabs("sp", SGD, sgd_base + 4)
                apo = bass.AP(dscr.tensor, r_sd,
                              [[DM, 128], [128 * DM, 4], [512, 2], [1, 512]])
                SPD.inc(sp.dma_start(out=apo, in_=stg_d[:, 0:4096]), 16)
                k.wait("sp", SPD, 0)
                sp.reg_add(r_sd, r_sd, SROW)

            def obp_body():
                k.forloop(2, ["sp"], tcq_body, extra={SGD: 4})
                sp.reg_add(r_sd, r_sd, 1024 - 2 * SROW)

            def e_body():
                k.forloop(2, ["sp"], obp_body, extra={SGD: 8})
                sp.reg_add(r_sd, r_sd, 2 * SROW - 2048)

            k.forloop(NE, ["sp"], e_body, extra={SGD: 16})
            if not _DRY[0]:
                SP_.reg_add(r_sd, r_sd, -NE * 1024 * DM)

        def pool_acc(spd_base):
            k.wait("gp", STC0, 0)
            k.wait("gp", STC1, 0)
            if not _DRY[0]:
                GP_.reg_mov(r_uvo, 0)

            def e_body():
                k.waitabs("gp", SPD, spd_base + 64)
                api = bass.AP(dscr.tensor, r_uvo, [[2 * DM, 512], [1, DM]])
                apo = bass.AP(out.tensor, r_so, [[2 * DM, 512], [1, DM]])
                STD.inc(gp.dma_start(out=apo, in_=api, accum_op=ALU.add), 16)
                gp.reg_add(r_uvo, r_uvo, DM)
                gp.reg_add(r_so, r_so, DM)
                api2 = bass.AP(dscr.tensor, r_uvo, [[2 * DM, 512], [1, DM]])
                apo2 = bass.AP(out.tensor, r_so, [[2 * DM, 512], [1, DM]])
                STD.inc(gp.dma_start(out=apo2, in_=api2, accum_op=ALU.add), 16)
                gp.reg_add(r_so, r_so, -DM)
                k.wait("gp", STD, 0)
                gp.reg_add(r_uvo, r_uvo, 1024 * DM - DM)

            k.forloop(NE, ["gp"], e_body, extra={SPD: 64})
            if not _DRY[0]:
                GP_.reg_add(r_so, r_so, SEG * DM)

        gp.wait_ge(LC.h, LC.n)
        k.barrier()

        def prebody():
            lv0 = LV.n
            lq0 = [LQ0.n, LQ1.n]
            if not _DRY[0]:
                GP_.reg_mov(r_uvo, 0)
                GP_.reg_mov(r_uqg, 0)
            poolB_stream()
            phaseA(x_pre)
            phaseB(True, lv0, lq0)
            k.barrier()

        if "P" in parts:
            k.forloop(NPRE, ["pe", "dv", "sc", "gp", "sp"], prebody)

        SP_.reg_mov(r_xrow, 0)

        def segbody():
            lv0 = LV.n
            lq0 = [LQ0.n, LQ1.n]
            ld0 = [LD0.n, LD1.n]
            spd0 = SPD.n
            sgd0 = SGD.n
            if not _DRY[0]:
                GP_.reg_mov(r_uvo, 0)
                GP_.reg_mov(r_uqg, 0)
                GP_.reg_mov(r_uwd, 0)
                DV_.reg_mov(r_cw, 0)
            if "B" in parts:
                poolB_stream()
            if "C" in parts:
                pool_stream(4, wvo_th, CI * 512, r_uvo, [LV], FWV, [wvb[:]])

            def poolD():
                pool_stream(6, wpr_th, 4352, r_uqg, [LQ0, LQ1], FWQ,
                            [wqg[:, 0:4352], wqg[:, 4352:8704]])
                pool_stream(2, wd_th, FCN * 512, r_uwd, [LD0, LD1], FWD,
                            [wdb[:, 0:FCN * 512], wdb[:, FCN * 512:2 * FCN * 512]])

            if "D" in parts:
                k.forloop(NE, ["gp"], poolD, extra={FWQ: 12, FWD: 4})
            if "A" in parts:
                phaseA(x_own)
            if "B" in parts:
                phaseB(False, lv0, lq0)
            if "C" in parts:
                phaseC(lv0)
            if "R" in parts:
                router()
            if "D" in parts:
                phaseD(lq0, ld0, spd0)
            if "S" in parts:
                sp_dstores(sgd0)
                pool_acc(spd0)
            if "C" in parts and not _DRY[0]:
                SP_.reg_add(r_cst, r_cst, SEG * DM - 4 * 512)
            k.barrier()

        k.forloop(NSEG, ["pe", "dv", "sc", "gp", "sp"], segbody)

        if "C" in parts:
            gp.wait_ge(STC0.h, STC0.n)
            gp.wait_ge(STC1.h, STC1.n)
        for off in (0, DM):
            apo = bass.AP(out.tensor, off, [[2 * DM, TOK // 2], [1, DM]])
            api = bass.AP(x_own.tensor, off, [[2 * DM, TOK // 2], [1, DM]])
            STD.inc(gp.dma_start(out=apo, in_=api, accum_op=ALU.add), 16)
        gp.wait_ge(STD.h, STD.n)
    return nc



def _stage_weights(inp):
    f32 = np.float32
    asn = lambda a: np.ascontiguousarray(a, dtype=f32)

    Wq, Wk = np.asarray(inp["L0_attn_Wq"]), np.asarray(inp["L0_attn_Wk"])
    Wv, Wo = np.asarray(inp["L0_attn_Wv"]), np.asarray(inp["L0_attn_Wo"])
    Wfq, Wfk = np.asarray(inp["L0_attn_Wfq"]), np.asarray(inp["L0_attn_Wfk"])
    gwt = np.asarray(inp["L0_ffn_gate_weight"])
    wg, wu, wd = (np.asarray(inp["L0_ffn_w_gate"]), np.asarray(inp["L0_ffn_w_up"]),
                  np.asarray(inp["L0_ffn_w_down"]))
    sg, su, sd = (np.asarray(inp["L0_ffn_sw_gate"]), np.asarray(inp["L0_ffn_sw_up"]),
                  np.asarray(inp["L0_ffn_sw_down"]))

    def hp_tiles(W):
        return W.T.reshape(CI, 128, 16, 128).transpose(2, 1, 0, 3)

    wqk = np.stack([hp_tiles(Wq), hp_tiles(Wk)], axis=2).reshape(16, 128, 2 * CI * 128)
    wftail = np.zeros((16, 128, 256), dtype=f32)
    for hp in range(16):
        blk = np.zeros((64, 256), dtype=f32)
        for hloc in range(2):
            blk[:, hloc * 64:(hloc + 1) * 64] = Wfq[hp * 2 + hloc].astype(f32)
            blk[:, 128 + hloc * 64:128 + (hloc + 1) * 64] = Wfk[hp * 2 + hloc].astype(f32)
        wftail[hp, 0:64] = blk
        wftail[hp, 64:128] = blk
    wqk = np.concatenate([wqk, wftail], axis=2)

    def gu_tiles(g, u):
        gT = g.T.reshape(CI, 128, FCN, 128).transpose(2, 1, 0, 3)
        uT = u.T.reshape(CI, 128, FCN, 128).transpose(2, 1, 0, 3)
        return np.stack([gT, uT], axis=2).reshape(FCN, 128, 2 * CI * 128)

    wgu_list = [gu_tiles(wg[e], wu[e]) for e in range(8)]
    wgu_list.append(gu_tiles(sg[0:1408], su[0:1408]))
    wgu_list.append(gu_tiles(sg[1408:2816], su[1408:2816]))
    wgu_list = [np.concatenate([g, g[0:1]], axis=0) for g in wgu_list]
    wgu_all = np.concatenate(wgu_list, axis=0)
    wgu_all = np.concatenate([wgu_all, np.zeros((120, 128, 256), dtype=f32)], axis=2)
    wpr_t = asn(np.concatenate([wqk, wgu_all], axis=0))

    def big_tiles(W):
        return W.T.reshape(CI, 128, 4, 512).transpose(2, 1, 0, 3).reshape(4, 128, CI * 512)

    wvo_t = asn(np.concatenate([big_tiles(Wv), big_tiles(Wo)], axis=0))

    def d_tiles(W):
        return W.T.reshape(FCN, 128, OBN, 512).transpose(2, 1, 0, 3).reshape(OBN, 128, FCN * 512)

    wd_list = [d_tiles(wd[e]) for e in range(8)]
    wd_list.append(d_tiles(sd[:, 0:1408]))
    wd_list.append(d_tiles(sd[:, 1408:2816]))
    wd_t = asn(np.concatenate(wd_list, axis=0))

    gw_t = asn(gwt.T.reshape(CI, 128, 8).transpose(1, 0, 2).reshape(128, CI * 8))
    ident = np.eye(128, dtype=ml_dtypes.bfloat16)
    maskut = np.tile(np.triu(np.ones((128, 128))), (1, 2)).astype(ml_dtypes.bfloat16)
    cva = np.tile(np.asarray([CONST_VALS], dtype=np.float32), (128, 1))

    return dict(wvo_t=wvo_t, wpr_t=wpr_t, wd_t=wd_t, gw_t=gw_t,
                ident=ident, maskut=maskut, cvals=cva)


_NC_CACHE = [None]


def kernel(**inputs):
    x = np.ascontiguousarray(np.asarray(inputs["L0_attn_hidden_states"], dtype=np.float32))
    w = _stage_weights(inputs)
    if _NC_CACHE[0] is None:
        _NC_CACHE[0] = build_nc()
    nc = _NC_CACHE[0]
    in_maps = []
    for core in range(8):
        b, half = core // 2, core % 2
        m = dict(w)
        m["x_own"] = np.ascontiguousarray(x[b, half * TOK:(half + 1) * TOK, :])
        m["x_pre"] = np.ascontiguousarray(x[b, 0:TOK, :])
        m["flagd"] = np.full((128, 1), float(half), dtype=np.float32)
        in_maps.append(m)
    res = run_bass_kernel_spmd(nc, in_maps, core_ids=list(range(8)))
    outp = np.empty((B, Lq, DM), dtype=np.float32)
    for core in range(8):
        b, half = core // 2, core % 2
        outp[b, half * TOK:(half + 1) * TOK, :] = np.asarray(res.results[core]["out"])
    return outp
